# version 04
# baseline (speedup 1.0000x reference)
import numpy as np
import ml_dtypes
import re as _re
import concourse.bass as bass
from concourse.bass_primitives_rust import RegisterHandle as _RegisterHandle
import concourse.mybir as mybir
from concourse.bass_utils import run_bass_kernel_spmd

F32 = mybir.dt.float32
BF16 = mybir.dt.bfloat16
I32 = mybir.dt.int32
AF = mybir.ActivationFunctionType
ALU = mybir.AluOpType
AX = mybir.AxisListType
IOA = bass.IndirectOffsetOnAxis

D = 1024
NB = 4
S = 2048
T = NB * S
NT = T // 128
CAP = int(8 * T / 8 * 1.25) + 1
EPS_LN = 1e-5
EPS_ATTN = 1e-6
NCORE = 8

ENGS = ("tensor", "vector", "scalar", "gpsimd", "sync")


class Em:

    DMA_ROT = 12

    def __init__(self, nc, paranoid=False):
        self.paranoid = paranoid
        self.nc = nc
        self.ops = {e: [] for e in ENGS}
        self.sems = {}
        self.counts = {}
        self.floors = {e: {} for e in ENGS}
        self.lastw = {}
        self.readers = {}
        for e in ENGS:
            self._mk("p_" + e)
        self.dmarot = {}
        for q in ("sync", "scalar", "gpsimd", "vector", "cc"):
            for i in range(self.DMA_ROT):
                self._mk(f"d_{q}_{i}")
            self.dmarot[q] = 0

    def _mk(self, name):
        self.sems[name] = self.nc.alloc_semaphore(name)
        self.counts[name] = 0

    def op(self, eng, fn, reads=(), writes=(), dma=False, queue=None, cc=False, n_dma=1, free_regs=False):
        waits = {}

        def need(s, v):
            if v > waits.get(s, -1):
                waits[s] = v

        if cc or dma:
            q = "cc" if cc else (queue or eng)
            i = self.dmarot[q]
            self.dmarot[q] = (i + 1) % self.DMA_ROT
            semname = f"d_{q}_{i}"
            inc = 1 if cc else 16 * n_dma
            if self.counts[semname] > 0:
                need(semname, self.counts[semname])
        else:
            semname = "p_" + eng
            inc = 1

        for k in reads:
            for sn, v in self.lastw.get(k, {}).items():
                need(sn, v)
            if k.startswith("ps"):
                for sn, v in self.readers.get(k, {}).items():
                    need(sn, v)
        for k in writes:
            for sn, v in self.lastw.get(k, {}).items():
                need(sn, v)
            for sn, v in self.readers.get(k, {}).items():
                need(sn, v)
        wl = []
        fl = self.floors[eng]
        for s, v in sorted(waits.items()):
            if self.paranoid or v > fl.get(s, 0):
                wl.append((s, v))
                fl[s] = max(v, fl.get(s, 0))
        self.counts[semname] += inc
        newv = self.counts[semname]
        self.ops[eng].append((wl, fn, semname, inc, free_regs))
        rec = dict(waits)
        rec[semname] = newv
        for k in writes:
            self.lastw[k] = dict(rec)
            self.readers[k] = {}
        for k in reads:
            r = self.readers.setdefault(k, {})
            for sn, v in rec.items():
                if v > r.get(sn, -1):
                    r[sn] = v
        return (semname, newv)

    def barrier_key(self, key):
        return key

    ENG_TYPE = {"gpsimd": mybir.EngineType.Pool, "vector": mybir.EngineType.DVE,
                "scalar": mybir.EngineType.Activation, "tensor": mybir.EngineType.PE,
                "sync": mybir.EngineType.SP}

    def replay(self, eng, e):
        et = self.ENG_TYPE[eng]
        for wl, fn, semname, inc, free_regs in self.ops[eng]:
            for s, v in wl:
                e.wait_ge(self.sems[s], v)
            ins = fn(e)
            insl = ins if isinstance(ins, list) else ([] if ins is None else [ins])
            if isinstance(ins, list):
                for i2 in insl:
                    i2.then_inc(self.sems[semname], 16)
            elif ins is not None:
                ins.then_inc(self.sems[semname], inc)
            if free_regs:
                for i2 in insl:
                    for nm in set(_re.findall(r"R\[([A-Za-z_0-9]+)\]", str(i2.ins))):
                        e.free_register(_RegisterHandle(nm, et))

    def finish(self, eng, e):
        for s, c in self.counts.items():
            if c > 0:
                e.wait_ge(self.sems[s], c)


_PARANOID = False
_STAGES = 99
_SUB = 99
_L1P = 99
_L1A = 99


def build(core_weight_names):
    nc = bass.Bass("TRN2", target_bir_lowering=False, dynamic_dma_scratch_size=16384)
    em = Em(nc, paranoid=_PARANOID)

    x_in = nc.dram_tensor("x", [T, D], F32, kind="ExternalInput").ap()
    y_out = nc.dram_tensor("y", [T, D], F32, kind="ExternalOutput").ap()
    W = {}
    for name, shape in core_weight_names.items():
        W[name] = nc.dram_tensor(name, list(shape), F32, kind="ExternalInput").ap()

    xbuf = nc.dram_tensor("xbuf", [T, D], F32, kind="Internal").ap()
    stg = {
        "l0w1": nc.dram_tensor("stg_l0w1", [4096, D], BF16, kind="Internal").ap(),
        "l0w2": nc.dram_tensor("stg_l0w2", [D, 4096], BF16, kind="Internal").ap(),
        "wq": nc.dram_tensor("stg_wq", [D, D], BF16, kind="Internal").ap(),
        "wk": nc.dram_tensor("stg_wk", [D, D], BF16, kind="Internal").ap(),
        "wv": nc.dram_tensor("stg_wv", [D, D], BF16, kind="Internal").ap(),
        "wo": nc.dram_tensor("stg_wo", [D, D], BF16, kind="Internal").ap(),
        "l1wq": nc.dram_tensor("stg_l1wq", [D, D], BF16, kind="Internal").ap(),
        "l1wk": nc.dram_tensor("stg_l1wk", [D, D], BF16, kind="Internal").ap(),
        "l1wv": nc.dram_tensor("stg_l1wv", [D, D], BF16, kind="Internal").ap(),
        "l1w1": nc.dram_tensor("stg_l1w1", [4096, D], BF16, kind="Internal").ap(),
        "l1w2": nc.dram_tensor("stg_l1w2", [D, 4096], BF16, kind="Internal").ap(),
    }
    rowbounce = nc.dram_tensor("rowbounce", [8, 1024], F32, kind="Internal").ap()
    rowb128 = nc.dram_tensor("rowb128", [8, 128], F32, kind="Internal").ap()
    xb_d = nc.dram_tensor("xb_d", [T, D], F32, kind="Internal").ap()
    cc_in = nc.dram_tensor("cc_in", [128, 128], F32, kind="Internal").ap()
    cc_out = nc.dram_tensor("cc_out", [128, 128], F32, kind="Internal").ap()
    cc_in2 = nc.dram_tensor("cc_in2", [128, 128], F32, kind="Internal").ap()
    cc_out2 = nc.dram_tensor("cc_out2", [128, 128], F32, kind="Internal").ap()
    xlnT_d = nc.dram_tensor("xlnT_d", [D, T], BF16, kind="Internal").ap()
    s_qT = nc.dram_tensor("s_qT", [D, S], BF16, kind="Internal").ap()
    s_kT = nc.dram_tensor("s_kT", [D, S], BF16, kind="Internal").ap()
    s_v = nc.dram_tensor("s_v", [S, D], BF16, kind="Internal").ap()
    xln_d = nc.dram_tensor("xln_d", [T, D], BF16, kind="Internal").ap()
    ysort_d = nc.dram_tensor("ysort_d", [24 * 512, D], F32, kind="Internal").ap()
    tokmap_d = nc.dram_tensor("tokmap_d", [24 * 512, 4], I32, kind="Internal").ap()
    b1T_d = nc.dram_tensor("b1T_d", [1024, 32], F32, kind="Internal").ap()
    b2w_d = nc.dram_tensor("b2w_d", [8, 1024], BF16, kind="Internal").ap()
    moew1_d = nc.dram_tensor("moew1_d", [8192, 4096], BF16, kind="Internal").ap()
    moew2_d = nc.dram_tensor("moew2_d", [32768, 1024], BF16, kind="Internal").ap()

    identb = nc.inline_tensor(np.eye(128, dtype=ml_dtypes.bfloat16), "c_identb").ap()
    identf = nc.inline_tensor(np.eye(128, dtype=np.float32), "c_identf").ap()
    tu = np.triu(np.ones((128, 128), np.float32), 1).astype(ml_dtypes.bfloat16)
    tstrict = nc.inline_tensor(tu, "c_tstrict").ap()
    ones_1x128_b = nc.inline_tensor(np.ones((1, 128), ml_dtypes.bfloat16), "c_ones1b").ap()
    ones_1x128_f = nc.inline_tensor(np.ones((1, 128), np.float32), "c_ones1f").ap()
    ones_128x1_b = nc.inline_tensor(np.ones((128, 1), ml_dtypes.bfloat16), "c_ones128b").ap()
    ones_128x1_f = nc.inline_tensor(np.ones((128, 1), np.float32), "c_ones128f").ap()
    iota8_rows = nc.inline_tensor(np.tile(np.arange(8, dtype=np.float32), (128, 1)), "c_iota8").ap()
    rev8_rows = nc.inline_tensor(np.tile((8.0 - np.arange(8, dtype=np.float32)), (128, 1)), "c_rev8").ap()
    iota128_col = nc.inline_tensor(np.arange(128, dtype=np.float32).reshape(128, 1), "c_iota128").ap()
    bndi_c = nc.inline_tensor(np.array([[T - 1, 24 * 512 - 1]], np.int32), "c_bndi").ap()
    tri8f = nc.inline_tensor(np.triu(np.ones((8, 8), np.float32), 1), "c_tri8f").ap()
    gs24 = nc.inline_tensor((np.arange(24, dtype=np.float32) * 512).reshape(24, 1), "c_gs24").ap()
    _bidx = np.zeros((128, 40), np.float32)
    _midx = np.zeros((128, 40), np.float32)
    _p = np.arange(128, dtype=np.float32)[:, None]
    for _c in range(8):
        _bidx[:, _c:_c + 1] = _c * 128 + _p
        _midx[:, _c] = 1024.0
    for _f in range(32):
        _bidx[:, 8 + _f:9 + _f] = _f * 128 + _p
        _midx[:, 8 + _f] = 4096.0
    bidxw = nc.inline_tensor(_bidx, "c_bidxw").ap()
    multw = nc.inline_tensor(_midx, "c_multw").ap()

    A = nc.alloc_sbuf_tensor
    BREG = {}

    X = [A(f"Xs{i}", [128, D], F32) for i in range(4)]
    XLN = [A("XLN0", [128, D], BF16)]
    XLT = [A("XLT0", [128, 2048], BF16)]
    H = A("H", [128, 4096], BF16)
    WB1 = A("WB1", [128, 32768], BF16)
    WB2 = A("WB2", [128, 32768], BF16)
    SUMS = A("SUMS", [128, NT], F32)
    SUMSQ = A("SUMSQ", [128, NT], F32)
    MU = A("MU", [128, NT], F32)
    RSTD = A("RSTD", [128, NT], F32)
    CONSTC = A("CONSTC", [128, 8], F32)
    THF = A("THF", [128, 2048], F32)
    RW = A("RW", [128, 8], F32)
    OHA = A("OHA", [128, 512], F32)
    RANKA = A("RANKA", [128, 64], F32)
    COMB = A("COMB", [128, 64], F32)
    COEF = A("COEF", [128, 64], F32)
    SLOTI = A("SLOTI", [128, 64], I32)
    LG = A("LG", [128, 8], F32)
    OHB = A("OHB", [128, 8], BF16)
    TMP8 = A("TMP8", [128, 8], F32)
    RUNOFF = A("RUNOFF", [1, 8], F32)
    RUNOFB = A("RUNOFB", [1, 8], F32)
    MROW = A("MROW", [8, 8], F32)
    SEGS = A("SEGS", [24, 8], F32)
    ROW8 = A("ROW8", [1, 8], F32)
    NV8 = A("NV8", [1, 8], F32)
    SS8 = A("SS8", [1, 8], F32)
    I8 = A("I8", [1, 8], I32)
    PCOL = A("PCOL", [8, 1], F32)
    NVB = A("NVB", [128, 8], F32)
    SEGB = A("SEGB", [128, 8], F32)
    OFSB = A("OFSB", [128, 8], F32)
    PIDC = A("PIDC", [128, 1], F32)
    MKC = A("MKC", [128, 1], F32)
    ZB = A("ZB", [128, 128], F32)
    TMAPV = A("TMAPV", [128, 4], I32)
    IDX4 = A("IDX4", [128, 4], I32)
    EB = [A(f"EB{i}", [128, 1024], BF16) for i in range(2)]
    ETABR = A("ETABR", [1, 24], I32)
    TRI8 = A("TRI8", [8, 8], F32)
    GS24T = A("GS24T", [24, 1], F32)
    WGT = A("WGT", [128, 64], BF16)
    B1M = A("B1M", [128, 256], F32)
    PIDT = A("PIDT", [1, 1], mybir.dt.uint32)
    TM96 = A("TM96", [128, 384], I32)
    IDX4B = A("IDX4B", [128, 4], I32)
    B1CUR = A("B1CUR", [128, 32], F32)
    B2CUR = A("B2CUR", [2, 1024], BF16)
    BIDXW = A("BIDXW", [128, 40], F32)
    MULTW = A("MULTW", [128, 40], F32)
    IDXWF = A("IDXWF", [128, 40], F32)
    IDXWI = A("IDXWI", [128, 40], I32)
    EGB = A("EGB", [128, 24], F32)
    ETABF = A("ETABF", [1, 24], F32)
    IDXB = A("IDXB", [128, 1], I32)
    IDXB2 = A("IDXB2", [2, 1], I32)
    BNDI = A("BNDI", [1, 2], I32)
    B1C_L0 = A("B1C_L0", [128, 32], F32)
    B2R_L0 = A("B2R_L0", [1, D], BF16)
    BQC = A("BQC", [128, 8], F32)
    BQE = A("BQE", [128, 8], F32)
    B1EC = A("B1EC", [128, 32], F32)
    B2R_L1 = A("B2R_L1", [1, D], BF16)
    BKER = A("BKER", [1, D], BF16)
    BVER = A("BVER", [1, D], BF16)
    BKEC = A("BKEC", [128, 8], F32)
    BVEC = A("BVEC", [128, 8], F32)
    XR = A("XR", [128, D], F32)
    KVSB = A("KVSB", [128, 1040], BF16)
    KSUM = A("KSUM", [128, 8], BF16)
    ZT = A("ZT", [128, 32], F32)
    DENS = THF
    BKC = A("BKC", [128, 8], F32)
    BVR = A("BVR", [1, D], BF16)
    BOR = A("BOR", [1, D], BF16)
    PF = [X[0], X[1]]
    PFK = ["X0", "X1"]
    PB0 = XLN[0]
    PB0K = "XLN0"
    PS_SC = A("PS_SC", [128, 4], F32)
    IDB = A("IDB", [128, 128], BF16)
    IDF = A("IDF", [128, 128], F32)
    TSTR = A("TSTR", [128, 128], BF16)
    ONE1B = A("ONE1B", [1, 128], BF16)
    ONE1F = A("ONE1F", [1, 128], F32)
    ONEC_B = A("ONEC_B", [128, 1], BF16)
    ONEC_F = A("ONEC_F", [128, 1], F32)
    IOTA8 = A("IOTA8", [128, 8], F32)
    REV8 = A("REV8", [128, 8], F32)
    IOTA128 = A("IOTA128", [128, 1], F32)
    SQJ = XLN[0]; SQJK = "XLN0"

    psS = nc.alloc_psum_tensor("psS", [128, 2048], F32)
    psT = nc.alloc_psum_tensor("psT", [128, 1024], BF16)
    psU = nc.alloc_psum_tensor("psU", [128, 512], F32)
    psV = nc.alloc_psum_tensor("psV", [128, 512], F32)
    psW = nc.alloc_psum_tensor("psW", [128, 512], F32)

    op = em.op

    def load_consts():
        op("sync", lambda e: e.dma_start(out=IDB[:, :], in_=identb[:, :]), writes=["IDB"], dma=True)
        op("sync", lambda e: e.dma_start(out=TSTR[:, :], in_=tstrict[:, :]), writes=["TSTR"], dma=True)
        op("sync", lambda e: e.dma_start(out=IDF[:, :], in_=identf[:, :]), writes=["IDF"], dma=True)
        op("sync", lambda e: e.dma_start(out=ONE1B[:, :], in_=ones_1x128_b[:, :]), writes=["ONE1B"], dma=True)
        op("sync", lambda e: e.dma_start(out=ONE1F[:, :], in_=ones_1x128_f[:, :]), writes=["ONE1F"], dma=True)
        op("sync", lambda e: e.dma_start(out=ONEC_B[:, :], in_=ones_128x1_b[:, :]), writes=["ONEC_B"], dma=True)
        op("sync", lambda e: e.dma_start(out=ONEC_F[:, :], in_=ones_128x1_f[:, :]), writes=["ONEC_F"], dma=True)
        op("sync", lambda e: e.dma_start(out=IOTA8[:, :], in_=iota8_rows[:, :]), writes=["IOTA8"], dma=True)
        op("sync", lambda e: e.dma_start(out=REV8[:, :], in_=rev8_rows[:, :]), writes=["REV8"], dma=True)
        op("sync", lambda e: e.dma_start(out=IOTA128[:, :], in_=iota128_col[:, :]), writes=["IOTA128"], dma=True)
        op("vector", lambda e: e.memset(CONSTC[:, 0:1], EPS_LN), writes=["CONSTC"])
        op("vector", lambda e: e.memset(CONSTC[:, 1:2], 1.0), reads=["CONSTC"], writes=["CONSTC"])
        op("vector", lambda e: e.memset(CONSTC[:, 2:3], EPS_ATTN), reads=["CONSTC"], writes=["CONSTC"])

    def stats_tile(xt_key, xt_tile, j):
        op("scalar",
           lambda e, t=xt_tile, jj=j: e.activation(SQJ[:, :], t[:, :], AF.Square,
                                                   accum_out=SUMSQ[:, jj:jj + 1]),
           reads=[xt_key], writes=[SQJK, f"sumsq{j}"])
        op("vector",
           lambda e, t=xt_tile, jj=j: e.reduce_sum(SUMS[:, jj:jj + 1], t[:, :], axis=AX.X),
           reads=[xt_key], writes=[f"sums{j}"])

    def ln_finalize():
        rd = [f"sums{j}" for j in range(NT)] + [f"sumsq{j}" for j in range(NT)]
        op("vector", lambda e: e.tensor_scalar(MU[:, :], SUMS[:, :], 1.0 / D, None, ALU.mult),
           reads=rd, writes=["MU"])
        op("scalar", lambda e: e.activation(RSTD[:, :], MU[:, :], AF.Square),
           reads=["MU"], writes=["RSTD"])
        op("vector", lambda e: e.scalar_tensor_tensor(RSTD[:, :], SUMSQ[:, :], 1.0 / D,
                                                      RSTD[:, :], ALU.mult, ALU.subtract),
           reads=rd + ["RSTD"], writes=["RSTD"])
        op("scalar", lambda e: e.activation(RSTD[:, :], RSTD[:, :], AF.Ln, bias=CONSTC[:, 0:1]),
           reads=["RSTD", "CONSTC"], writes=["RSTD"])
        op("scalar", lambda e: e.activation(RSTD[:, :], RSTD[:, :], AF.Exp, scale=-0.5),
           reads=["RSTD"], writes=["RSTD"])

    def ln_apply(xk, xt, j, outk, outt):
        op("vector",
           lambda e, t=xt, o=outt, jj=j: e.tensor_scalar(
               o[:, :], t[:, :], MU[:, jj:jj + 1], RSTD[:, jj:jj + 1], ALU.subtract, ALU.mult),
           reads=[xk, "MU", "RSTD"], writes=[outk])

    def transpose_tile(srck, srct, dstk, dstt, dst_cols, tok_off, tok_w=256):
        def tfn(e, s=srct):
            ins = None
            for c in range(8):
                ins = e.transpose(psT[:, c * 128:(c + 1) * 128], s[:, c * 128:(c + 1) * 128], IDB[:, :])
            return ins
        op("tensor", tfn, reads=[srck, "IDB"], writes=["psT"])

        def evfn(e, d=dstt):
            ins = None
            for c in range(8):
                ins = e.tensor_copy(d[:, dst_cols(c) + tok_off: dst_cols(c) + tok_off + 128],
                                    psT[:, c * 128:(c + 1) * 128])
            return ins
        op("vector", evfn, reads=["psT"], writes=[dstk])

    def load_x_slab(sl, xi, src=None, srcname="xbuf"):
        if src is None:
            src = xbuf
        for k in range(2):
            j = sl * 2 + k
            op("scalar",
               lambda e, jj=j, t=X[xi + k], s0=src: e.dma_start(
                   out=t[:, :], in_=s0[jj * 128:(jj + 1) * 128, :]),
               reads=[f"{srcname}{j}"], writes=[f"X{xi + k}"], dma=True, queue="scalar")

    def store_x_slab(sl, xi, dest, destname):
        for k in range(2):
            j = sl * 2 + k
            op("scalar",
               lambda e, jj=j, t=X[xi + k]: e.dma_start(
                   out=dest[jj * 128:(jj + 1) * 128, :], in_=t[:, :]),
               reads=[f"X{xi + k}"], writes=[f"{destname}{j}"], dma=True, queue="scalar")

    def prep_cast(name_src, ap_src, stg_key, rows):
        dst = stg[stg_key]
        op("gpsimd", lambda e: e.dma_start(out=dst[:, :], in_=ap_src[:, :]),
           writes=[f"stg_{stg_key}"], dma=True)

    def prep_ternary(src_ap, stg_key, shape):
        nr, ncol = shape[0] // 128, shape[1] // 1024
        tiles = [(r, c) for r in range(nr) for c in range(ncol)]
        sck = "PS_SC"
        op("vector", lambda e: e.memset(PS_SC[:, 0:1], 0.0), writes=[sck])
        for i, (r, c) in enumerate(tiles):
            pf = PF[i % 2]
            pfk = PFK[i % 2]
            op("sync", lambda e, rr=r, cc=c, t=pf: e.dma_start(
                out=t[:, :], in_=src_ap[rr * 128:(rr + 1) * 128, cc * 1024:(cc + 1) * 1024]),
               writes=[pfk], dma=True)
            op("vector", lambda e, t=pf: e.reduce_sum(PS_SC[:, 1:2], t[:, :], axis=AX.X,
                                                      apply_absolute_value=True),
               reads=[pfk], writes=["PS_SCt"])
            op("vector", lambda e: e.tensor_add(PS_SC[:, 0:1], PS_SC[:, 0:1], PS_SC[:, 1:2]),
               reads=["PS_SC", "PS_SCt"], writes=["PS_SC"])
        op("tensor", lambda e: e.matmul(psU[0:1, 0:1], PS_SC[:, 0:1], ONEC_F[:, :],
                                        start=True, stop=True),
           reads=[sck, "ONEC_F"], writes=["psU"])
        n_el = shape[0] * shape[1]
        op("vector", lambda e: e.tensor_scalar(PS_SC[0:1, 2:3], psU[0:1, 0:1], 1.0 / n_el, 1e-8,
                                               ALU.mult, ALU.max),
           reads=["psU"], writes=["PS_SCthr0"])
        op("vector", lambda e: e.tensor_scalar(PS_SC[0:1, 2:3], PS_SC[0:1, 2:3], 0.5, None, ALU.mult),
           reads=["PS_SCthr0"], writes=["PS_SCthr0"])
        op("tensor", lambda e: e.matmul(psU[:, 1:2], ONE1F[:, :], PS_SC[0:1, 2:3],
                                        start=True, stop=True),
           reads=["PS_SCthr0", "ONE1F"], writes=["psU"])
        op("vector", lambda e: e.tensor_copy(PS_SC[:, 3:4], psU[:, 1:2]),
           reads=["psU"], writes=["PS_SCthr"])
        dst = stg[stg_key]
        for (r, c) in tiles:
            op("sync", lambda e, rr=r, cc=c: e.dma_start(
                out=PF[0][:, :], in_=src_ap[rr * 128:(rr + 1) * 128, cc * 1024:(cc + 1) * 1024]),
               writes=[PFK[0]], dma=True)
            op("scalar", lambda e: e.sign(PF[1][:, :], PF[0][:, :]),
               reads=[PFK[0]], writes=[PFK[1]])
            op("scalar", lambda e: e.activation(PF[0][:, :], PF[0][:, :], AF.Abs),
               reads=[PFK[0], PFK[1]], writes=[PFK[0]])
            op("vector", lambda e: e.tensor_single_scalar(PF[0][:, :], PF[0][:, :],
                                                          PS_SC[:, 3:4], ALU.is_gt),
               reads=[PFK[0], "PS_SCthr"], writes=[PFK[0]])
            op("vector", lambda e: e.tensor_mul(PB0[:, :], PF[0][:, :], PF[1][:, :]),
               reads=[PFK[0], PFK[1]], writes=[PB0K])
            op("gpsimd", lambda e, rr=r, cc=c: e.dma_start(
                out=dst[rr * 128:(rr + 1) * 128, cc * 1024:(cc + 1) * 1024], in_=PB0[:, :]),
               reads=[PB0K], writes=[f"stg_{stg_key}"], dma=True)

    def prep_bias_col(src_ap, n, dstk, dstt):
        op("sync", lambda e: e.dma_start(out=PF[0][0:n, 0:128],
                                         in_=src_ap.rearrange("(c p) -> c p", p=128)),
           writes=[PFK[0]], dma=True)
        op("tensor", lambda e: e.transpose(psW[:, 0:n], PF[0][0:n, 0:128], IDF[0:n, 0:n]),
           reads=[PFK[0], "IDF"], writes=["psW"])
        op("vector", lambda e: e.tensor_copy(dstt[:, 0:n], psW[:, 0:n]),
           reads=["psW"], writes=[dstk])

    def prep_bias_row(src_ap, n, dstk, dstt):
        op("sync", lambda e: e.dma_start(out=PF[1][0:1, 0:n], in_=src_ap.rearrange("(o d) -> o d", o=1)),
           writes=[PFK[1]], dma=True)
        op("vector", lambda e: e.tensor_copy(dstt[0:1, 0:n], PF[1][0:1, 0:n]),
           reads=[PFK[1]], writes=[dstk])

    def prep_stage_cast(src_name, stg_key):
        dst = stg[stg_key]
        op("gpsimd", lambda e: e.dma_start(out=dst[:, :], in_=W[src_name][:, :]),
           writes=[f"stg_{stg_key}"], dma=True)

    def prep_attn_l0():
        for nm, key in (("L0_attn_Wq", "wq"), ("L0_attn_Wk", "wk"),
                        ("L0_attn_Wv", "wv"), ("L0_attn_Wo", "wo")):
            prep_stage_cast(nm, key)
        prep_bias_col(W["L0_attn_bq"], 8, "BQC", BQC)
        prep_bias_col(W["L0_attn_bk"], 8, "BKC", BKC)
        prep_bias_row(W["L0_attn_bv"], D, "BVR", BVR)
        prep_bias_row(W["L0_attn_bo"], D, "BOR", BOR)

    def prep_gscale(src_name, stg_key, g_half, n_rows):
        dst = stg[stg_key]
        for i in range(n_rows // 128):
            op("sync", lambda e, ii=i: e.dma_start(
                out=PF[0][:, :], in_=W[src_name][ii * 128:(ii + 1) * 128, :]),
               writes=[PFK[0]], dma=True)
            op("vector", lambda e, gh=g_half: e.tensor_mul(
                PF[0][:, :], PF[0][:, :], THF[:, gh * 1024:(gh + 1) * 1024]),
               reads=[PFK[0], f"THF{g_half}"], writes=[PFK[0]])
            op("vector", lambda e: e.tensor_copy(PB0[:, :], PF[0][:, :]),
               reads=[PFK[0]], writes=[PB0K])
            op("gpsimd", lambda e, ii=i: e.dma_start(
                out=dst[ii * 128:(ii + 1) * 128, :], in_=PB0[:, :]),
               reads=[PB0K], writes=[f"stg_{stg_key}"], dma=True)

    def prep_bcast_row(src_ap, g_half):
        op("sync", lambda e: e.dma_start(out=PF[1][0:1, 0:1024],
                                         in_=src_ap.rearrange("(o d) -> o d", o=1)),
           writes=[PFK[1]], dma=True)
        for half in range(2):
            op("tensor", lambda e, hh=half: e.matmul(
                psW[:, :], ONE1F[:, :], PF[1][0:1, hh * 512:(hh + 1) * 512],
                start=True, stop=True),
               reads=[PFK[1], "ONE1F"], writes=["psW"])
            op("vector", lambda e, hh=half, gh=g_half: e.tensor_copy(
                THF[:, gh * 1024 + hh * 512: gh * 1024 + (hh + 1) * 512], psW[:, :]),
               reads=["psW"], writes=[f"THF{g_half}"])

    def prep_bias_matvec(src_name, be_half, n_rows, dstt, dstk, addt=None, addk=None):
        nt = n_rows // 128
        for i in range(nt):
            op("sync", lambda e, ii=i: e.dma_start(
                out=PF[0][:, :], in_=W[src_name][ii * 128:(ii + 1) * 128, :]),
               writes=[PFK[0]], dma=True)
            op("vector", lambda e, gh=be_half: e.tensor_mul(
                PF[0][:, :], PF[0][:, :], THF[:, gh * 1024:(gh + 1) * 1024]),
               reads=[PFK[0], f"THF{be_half}"], writes=[PFK[0]])
            op("vector", lambda e, ii=i: e.reduce_sum(dstt[:, ii:ii + 1], PF[0][:, :], axis=AX.X),
               reads=[PFK[0]], writes=[dstk])
        if addt is not None:
            op("vector", lambda e, n0=nt: e.tensor_add(dstt[:, 0:n0], dstt[:, 0:n0], addt[:, 0:n0]),
               reads=[dstk, addk], writes=[dstk])

    def prep_row_from_col(colt, colk, n, dstt, dstk):
        op("tensor", lambda e: e.transpose(psW[0:n, 0:128], colt[:, 0:n], IDF[:, :]),
           reads=[colk, "IDF"], writes=["psW"])
        op("vector", lambda e: e.tensor_copy(PF[0][0:n, 0:128], psW[0:n, 0:128]),
           reads=["psW"], writes=[PFK[0]])
        op("sync", lambda e: e.dma_start(out=rowb128[0:n, :], in_=PF[0][0:n, 0:128]),
           reads=[PFK[0]], writes=["rowb128"], dma=True)
        op("sync", lambda e: e.dma_start(
            out=PF[1][0:1, 0:n * 128], in_=rowb128.rearrange("(o a) b -> o (a b)", o=1)),
           reads=["rowb128"], writes=[PFK[1]], dma=True)
        op("vector", lambda e: e.tensor_copy(dstt[0:1, 0:n * 128], PF[1][0:1, 0:n * 128]),
           reads=[PFK[1]], writes=[dstk])

    def prep_l1():
        prep_bcast_row(W["L1_attn_g1"], 0)
        prep_bcast_row(W["L1_attn_g2"], 1)
        prep_gscale("L1_attn_wq", "l1wq", 0, D)
        prep_gscale("L1_attn_wk", "l1wk", 0, D)
        prep_gscale("L1_attn_wv", "l1wv", 0, D)
        prep_gscale("L1_attn_w1", "l1w1", 1, 4096)
        prep_stage_cast("L1_attn_w2", "l1w2")
        prep_bcast_row(W["L1_attn_be1"], 0)
        prep_bcast_row(W["L1_attn_be2"], 1)
        prep_bias_matvec("L1_attn_wq", 0, D, BQE, "BQE")
        prep_bias_matvec("L1_attn_wk", 0, D, BKEC, "BKEC")
        prep_row_from_col(BKEC, "BKEC", 8, BKER, "BKER")
        prep_bias_matvec("L1_attn_wv", 0, D, BVEC, "BVEC")
        prep_row_from_col(BVEC, "BVEC", 8, BVER, "BVER")
        prep_bias_col(W["L1_attn_b1"], 32, "B1M", B1M)
        prep_bias_matvec("L1_attn_w1", 1, 4096, B1EC, "B1EC", B1M, "B1M")
        prep_bias_row(W["L1_attn_b2"], D, "B2R_L1", B2R_L1)

    def prep_moe():
        op("sync", lambda e: e.dma_start(out=TRI8[:, :], in_=tri8f[:, :]), writes=["TRI8"], dma=True)
        op("sync", lambda e: e.dma_start(out=GS24T[:, :], in_=gs24[:, :]), writes=["GS24T"], dma=True)
        op("sync", lambda e: e.dma_start(out=BIDXW[:, :], in_=bidxw[:, :]), writes=["BIDXW"], dma=True)
        op("sync", lambda e: e.dma_start(out=MULTW[:, :], in_=multw[:, :]), writes=["MULTW"], dma=True)
        for ex in range(8):
            op("gpsimd", lambda e, x0=ex: e.dma_start(
                out=moew1_d[x0 * 1024:(x0 + 1) * 1024, :],
                in_=W["L1_ffn_W1"][x0 * 1024:(x0 + 1) * 1024, :]),
               writes=["moew1_d"], dma=True)
            op("gpsimd", lambda e, x0=ex: e.dma_start(
                out=moew2_d[x0 * 4096:(x0 + 1) * 4096, :],
                in_=W["L1_ffn_W2"][x0 * 4096:(x0 + 1) * 4096, :]),
               writes=["moew2_d"], dma=True)
        op("gpsimd", lambda e: e.dma_start(out=b2w_d[:, :], in_=W["L1_ffn_b2"][:, :]),
           writes=["b2w_d"], dma=True)
        for c in range(8):
            op("gpsimd", lambda e, cc=c: e.dma_start(
                out=WGT[:, cc * 8:(cc + 1) * 8], in_=W["L1_ffn_wg"][cc * 128:(cc + 1) * 128, :]),
               reads=["WGT"], writes=["WGT"], dma=True)
        for ex in range(8):
            op("sync", lambda e, x0=ex: e.dma_start(
                out=PF[0][0:32, 0:128],
                in_=W["L1_ffn_b1"][x0:x0 + 1, :].rearrange("o (c p) -> (o c) p", p=128)),
               writes=[PFK[0]], dma=True)
            op("tensor", lambda e: e.transpose(psW[:, 0:32], PF[0][0:32, 0:128], IDF[0:32, 0:32]),
               reads=[PFK[0], "IDF"], writes=["psW"])
            op("vector", lambda e, x0=ex: e.tensor_copy(B1M[:, x0 * 32:(x0 + 1) * 32], psW[:, 0:32]),
               reads=["psW"], writes=["B1M"])
        assert nc.partition_id_tensor is not None
        pid_ap = nc.partition_id_tensor.ap()
        op("sync", lambda e: e.dma_start(out=PIDT[:, :], in_=pid_ap[0:1, 0:1]),
           writes=["PIDT"], dma=True)
        op("vector", lambda e: e.tensor_copy(TMP8[0:1, 0:1], PIDT[0:1, 0:1]),
           reads=["PIDT"], writes=["TMP8"])
        op("tensor", lambda e: e.matmul(psW[:, 0:1], ONE1F[:, :], TMP8[0:1, 0:1], start=True, stop=True),
           reads=["TMP8", "ONE1F"], writes=["psW"])
        op("vector", lambda e: e.tensor_copy(PIDC[:, :], psW[:, 0:1]), reads=["psW"], writes=["PIDC"])
        op("vector", lambda e: e.tensor_single_scalar(MKC[:, :], IOTA128[:, :], PIDC[:, 0:1], ALU.is_lt),
           reads=["IOTA128", "PIDC"], writes=["MKC"])
        for ex in range(8):
            op("sync", lambda e, x0=ex: e.dma_start(
                out=b1T_d[x0 * 128:(x0 + 1) * 128, :], in_=B1M[:, x0 * 32:(x0 + 1) * 32]),
               reads=["B1M"], writes=["b1T_d"], dma=True)
        op("vector", lambda e: e.memset(ZB[:, :], 0.0), writes=["ZB"])
        op("vector", lambda e: e.memset(TMAPV[:, :], 0), writes=["TMAPV"])
        op("vector", lambda e: e.memset(EB[0][:, :], 0.0), writes=["EB0"])
        op("vector", lambda e: e.memset(EB[1][:, :], 0.0), writes=["EB1"])


    def prep_l0():
        if _SUB >= 1 and _SUB != -2:
            prep_ternary(W["L0_ffn_W1"], "l0w1", (4096, 1024))
        if _SUB == -2:
            prep_ternary(W["L0_ffn_W2"], "l0w2", (1024, 4096))
        if _SUB >= 2:
            prep_ternary(W["L0_ffn_W2"], "l0w2", (1024, 4096))
        if _SUB >= 3:
            prep_bias_col(W["L0_ffn_b1"], 32, "B1C_L0", B1C_L0)
        if _SUB >= 4:
            prep_bias_row(W["L0_ffn_b2"], D, "B2R_L0", B2R_L0)

    def phase_load_x():
        for sl in range(NT // 2):
            xi = (sl % 2) * 2
            for k in range(2):
                j = sl * 2 + k
                op("scalar", lambda e, jj=j, t=X[xi + k]: e.dma_start(
                    out=t[:, :], in_=x_in[jj * 128:(jj + 1) * 128, :]),
                   writes=[f"X{xi + k}"], dma=True, queue="scalar")
                stats_tile(f"X{xi + k}", X[xi + k], j)
                op("scalar", lambda e, jj=j, t=X[xi + k]: e.dma_start(
                    out=xbuf[jj * 128:(jj + 1) * 128, :], in_=t[:, :]),
                   reads=[f"X{xi + k}"], writes=[f"xbuf{j}"], dma=True, queue="scalar")

    def phase_l0ffn(dest, destname):
        phase_ffn(dest, destname, "l0w1", "l0w2", B1C_L0, "B1C_L0", B2R_L0, "B2R_L0", 0.5)

    def phase_l1ffn(dest, destname):
        phase_ffn(dest, destname, "l1w1", "l1w2", B1EC, "B1EC", B2R_L1, "B2R_L1", 1.0,
                  src=xb_d, srcname="xb", extra=xbuf, extraname="xbuf")

    def phase_ffn(dest, destname, stg1, stg2, b1t, b1k, b2t, b2k, scale,
                  src=None, srcname="xbuf", extra=None, extraname=None):
        for c in range(8):
            op("sync", lambda e, cc=c: e.dma_start(
                out=WB1[:, cc * 4096:(cc + 1) * 4096],
                in_=stg[stg1][0:4096, cc * 128:(cc + 1) * 128], transpose=True),
               reads=[f"stg_{stg1}"], writes=[f"WB1_{c}"], dma=True)
        for f in range(32):
            op("sync", lambda e, ff=f: e.dma_start(
                out=WB2[:, ff * 1024:(ff + 1) * 1024],
                in_=stg[stg2][0:1024, ff * 128:(ff + 1) * 128], transpose=True),
               reads=[f"stg_{stg2}"], writes=[f"WB2_{f // 4}"], dma=True)
        ln_finalize()
        wb1keys = [f"WB1_{c}" for c in range(8)]
        wb2keys = [f"WB2_{r}" for r in range(8)]
        for sl in range(NT // 2):
            xi = (sl % 2) * 2
            li = sl % 2
            load_x_slab(sl, xi, src, srcname)
            for k in range(2):
                j = sl * 2 + k
                ln_apply(f"X{xi + k}", X[xi + k], j, "XLN0", XLN[0])
                transpose_tile("XLN0", XLN[0], "XLT0", XLT[0],
                               (lambda c: c * 256), k * 128)
            outps = {(0, 0): psS[:, 1024:1536], (0, 1): psS[:, 1536:2048],
                     (1, 0): psU[:, :], (1, 1): psV[:, :]}
            outpk = {(0, 0): "psS_b2", (0, 1): "psS_b3", (1, 0): "psU", (1, 1): "psV"}
            for half in range(2):
                for fi in range(16):
                    f = half * 16 + fi
                    hp3 = (half * 16 + fi) % 3
                    ps = (psS[:, 0:256], psS[:, 512:768], psW[:, 0:256])[hp3]
                    pk = ("psS_b0", "psS_b1", "psW")[hp3]
                    def hmm(e, ff=f, p=ps):
                        ins = None
                        for c in range(8):
                            ins = e.matmul(p, WB1[:, c * 4096 + ff * 128: c * 4096 + ff * 128 + 128],
                                           XLT[0][:, c * 256:(c + 1) * 256],
                                           start=(c == 0), stop=(c == 7))
                        return ins
                    op("tensor", hmm, reads=["XLT0"] + wb1keys, writes=[pk])
                    op("scalar", lambda e, ff=f, fj=fi, p=ps: e.activation(
                        H[:, fj * 256:(fj + 1) * 256], p, AF.Gelu, bias=B1C_L0[:, ff:ff + 1]),
                       reads=[pk, "B1C_L0"], writes=[f"H{fi}"])
                hkeys = [f"H{fi}" for fi in range(16)]
                for k in range(2):
                    for dh in range(2):
                        ps = outps[(k, dh)]
                        pk = outpk[(k, dh)]
                        def omm(e, kk=k, dd=dh, p=ps, hh=half):
                            ins = None
                            if hh == 0:
                                ins = e.matmul(p, ONE1B[:, :], B2R_L0[0:1, dd * 512:(dd + 1) * 512],
                                               start=True, stop=False)
                            for fi in range(16):
                                f = hh * 16 + fi
                                ins = e.matmul(p, H[:, fi * 256 + kk * 128: fi * 256 + kk * 128 + 128],
                                               WB2[:, f * 1024 + dd * 512: f * 1024 + dd * 512 + 512],
                                               start=False, stop=(hh == 1 and fi == 15))
                            return ins
                        rds = hkeys + wb2keys + ["ONE1B", "B2R_L0"]
                        if half == 1:
                            rds = rds + [pk]
                        op("tensor", omm, reads=rds, writes=[pk])
            for k in range(2):
                j = sl * 2 + k
                for dh in range(2):
                    ps = outps[(k, dh)]
                    pk = outpk[(k, dh)]
                    op("vector", lambda e, dd=dh, p=ps, xt=X[xi + k], sc=scale: e.scalar_tensor_tensor(
                        xt[:, dd * 512:(dd + 1) * 512], p, sc,
                        xt[:, dd * 512:(dd + 1) * 512], ALU.mult, ALU.add),
                       reads=[pk, f"X{xi + k}"], writes=[f"X{xi + k}"])
                if extra is not None:
                    op("scalar", lambda e, jj=j, x0=extra: e.dma_start(
                        out=XR[:, :], in_=x0[jj * 128:(jj + 1) * 128, :]),
                       reads=[f"{extraname}{j}", "XR"], writes=["XR"], dma=True, queue="scalar")
                    op("vector", lambda e, xt=X[xi + k]: e.tensor_add(xt[:, :], xt[:, :], XR[:, :]),
                       reads=[f"X{xi + k}", "XR"], writes=[f"X{xi + k}"])
                stats_tile(f"X{xi + k}", X[xi + k], j)
            store_x_slab(sl, xi, dest, destname)

    def wb1r(c0, c1):
        return [f"WB1_{r}" for r in range(c0 // 4096, (c1 + 4095) // 4096)]

    def wb2r(c0, c1):
        return [f"WB2_{r}" for r in range(c0 // 4096, (c1 + 4095) // 4096)]

    PSS_ALL = ["psS_b0", "psS_b1", "psS_b2", "psS_b3"]

    def stream_w_chunk(stg_key, col0, c):
        op("sync", lambda e, cc=c, co=col0 + c * 1024, ww=stg_key: e.dma_start(
            out=WB1[:, co:co + 1024],
            in_=stg[ww][0:1024, cc * 128:(cc + 1) * 128], transpose=True),
           reads=[f"stg_{stg_key}"], writes=wb1r(col0 + c * 1024, col0 + (c + 1) * 1024), dma=True)

    def phase_l0attn(dest, destname):
        ln_finalize()
        for b in range(NB):
            jb = b * 16
            if b == 0:
                for wi, wn in enumerate(["wq", "wk", "wv", "wo"]):
                    for c in range(8):
                        stream_w_chunk(wn, wi * 8192, c)
            else:
                for c in range(8):
                    stream_w_chunk("wq", 0, c)
                for c in range(4):
                    stream_w_chunk("wk", 8192, c)
            for sl in range(8):
                xi = (sl % 2) * 2
                for k in range(2):
                    j = jb + sl * 2 + k
                    op("scalar", lambda e, jj=j, t=X[xi + k]: e.dma_start(
                        out=t[:, :], in_=xbuf[jj * 128:(jj + 1) * 128, :]),
                       reads=[f"xbuf{j}"], writes=[f"X{xi + k}"], dma=True, queue="scalar")
                    ln_apply(f"X{xi + k}", X[xi + k], j, "XLN0", XLN[0])
                    transpose_tile("XLN0", XLN[0], "XLT0", XLT[0],
                                   (lambda c: c * 256), k * 128)
                for wi, (dst_d, bias_c, bias_k) in enumerate(
                        ((s_qT, BQC, "BQC"), (s_kT, BKC, "BKC"))):
                    for ec in range(8):
                        ps = psU if ec % 2 == 0 else psV
                        pk = "psU" if ec % 2 == 0 else "psV"
                        def qmm(e, w0=wi * 8192, ee=ec, p=ps):
                            ins = None
                            for c in range(8):
                                ins = e.matmul(p[:, 0:256],
                                               WB1[:, w0 + c * 1024 + ee * 128: w0 + c * 1024 + ee * 128 + 128],
                                               XLT[0][:, c * 256:(c + 1) * 256],
                                               start=(c == 0), stop=(c == 7))
                            return ins
                        op("tensor", qmm, reads=["XLT0"] + wb1r(wi * 8192, (wi + 1) * 8192), writes=[pk])
                        op("scalar", lambda e, ee=ec, p=ps, bc=bias_c: e.activation(
                            H[:, ee * 256:(ee + 1) * 256], p[:, 0:256], AF.Identity, bias=bc[:, ee:ee + 1]),
                           reads=[pk, bias_k], writes=[f"H{ec}"])
                    def qst(e, d=dst_d, ss=sl):
                        return [e.dma_start(out=d[ec * 128:(ec + 1) * 128, ss * 256:(ss + 1) * 256],
                                            in_=H[:, ec * 256:(ec + 1) * 256])
                                for ec in range(8)]
                    op("sync", qst, reads=[f"H{ec}" for ec in range(8)],
                       writes=[f"sqkT{wi}_{sl}"], dma=True, n_dma=8)
                for k in range(2):
                    for dh in range(2):
                        ps = psS[:, 1024 + dh * 512: 1536 + dh * 512]
                        pk = f"psS_b{2 + dh}"
                        def vmm(e, kk=k, dd=dh, p=ps):
                            ins = e.matmul(p, ONE1B[:, :], BVR[0:1, dd * 512:(dd + 1) * 512],
                                           start=True, stop=False)
                            for c in range(8):
                                ins = e.matmul(p, XLT[0][:, c * 256 + kk * 128: c * 256 + kk * 128 + 128],
                                               WB1[:, 16384 + c * 1024 + dd * 512: 16384 + c * 1024 + dd * 512 + 512],
                                               start=False, stop=(c == 7))
                            return ins
                        op("tensor", vmm, reads=["XLT0", "ONE1B", "BVR"] + wb1r(16384, 24576), writes=[pk])
                        op("vector", lambda e, dd=dh, p=ps: e.tensor_copy(
                            XLN[0][:, dd * 512:(dd + 1) * 512], p),
                           reads=[pk], writes=["XLN0"])
                    op("sync", lambda e, ss=sl, kk=k: e.dma_start(
                        out=s_v[ss * 256 + kk * 128: ss * 256 + kk * 128 + 128, :], in_=XLN[0][:, :]),
                       reads=["XLN0"], writes=[f"sv{sl}_{k}"], dma=True)
            for c in range(8):
                op("sync", lambda e, cc=c: e.dma_start(
                    out=WB2[:, cc * 2048:(cc + 1) * 2048], in_=s_kT[cc * 128:(cc + 1) * 128, :]),
                   reads=[f"sqkT1_{ss}" for ss in range(8)],
                   writes=wb2r(c * 2048, (c + 1) * 2048), dma=True)
            for t_ in range(16):
                op("sync", lambda e, tt=t_: e.dma_start(
                    out=WB2[:, 16384 + tt * 1024: 16384 + (tt + 1) * 1024],
                    in_=s_v[tt * 128:(tt + 1) * 128, :]),
                   reads=[f"sv{t_ // 2}_{t_ % 2}"],
                   writes=wb2r(16384 + t_ * 1024, 16384 + (t_ + 1) * 1024), dma=True)
            kT_keys = wb2r(0, 16384)
            v_keys = wb2r(16384, 32768)
            for g in range(4):
                for c in range(8):
                    op("sync", lambda e, cc=c, gg=g: e.dma_start(
                        out=WB1[:, cc * 512:(cc + 1) * 512],
                        in_=s_qT[cc * 128:(cc + 1) * 128, gg * 512:(gg + 1) * 512]),
                       reads=[f"sqkT0_{ss}" for ss in range(8)], writes=wb1r(0, 4096), dma=True)
                for ci in range(4):
                    def smm(e, cc=ci):
                        ins = None
                        for nb4 in range(4):
                            for c in range(8):
                                ins = e.matmul(psS[:, nb4 * 512:(nb4 + 1) * 512],
                                               WB1[:, c * 512 + cc * 128: c * 512 + cc * 128 + 128],
                                               WB2[:, c * 2048 + nb4 * 512: c * 2048 + nb4 * 512 + 512],
                                               start=(c == 0), stop=(c == 7))
                        return ins
                    op("tensor", smm, reads=wb1r(0, 4096) + kT_keys, writes=PSS_ALL)
                    for h2 in range(2):
                        xt = X[2 + h2]
                        xk = f"X{2 + h2}"
                        sl_ = slice(h2 * 1024, (h2 + 1) * 1024)
                        op("scalar", lambda e, t=xt, s0=sl_: e.activation(
                            t[:, :], psS[:, s0], AF.Tanh, scale=0.5),
                           reads=PSS_ALL, writes=[xk])
                        op("scalar", lambda e, s0=sl_, hh=h2: e.activation(
                            THF[:, hh * 1024:(hh + 1) * 1024], psS[:, s0], AF.Tanh),
                           reads=PSS_ALL, writes=[f"THF{h2}"])
                        op("vector", lambda e, t=xt: e.tensor_scalar(
                            t[:, :], t[:, :], 0.1, 0.6, ALU.mult, ALU.add),
                           reads=[xk], writes=[xk])
                        op("vector", lambda e, t=xt, hh=h2: e.scalar_tensor_tensor(
                            t[:, :], THF[:, hh * 1024:(hh + 1) * 1024], 0.15, t[:, :],
                            ALU.mult, ALU.add),
                           reads=[xk, f"THF{h2}"], writes=[xk])
                        op("scalar", lambda e, s0=sl_, hh=h2: e.activation(
                            THF[:, hh * 1024:(hh + 1) * 1024], psS[:, s0], AF.Relu, scale=0.1),
                           reads=PSS_ALL + [f"THF{h2}"], writes=[f"THF{h2}"])
                        op("gpsimd", lambda e, t=xt, hh=h2: e.tensor_add(
                            t[:, :], t[:, :], THF[:, hh * 1024:(hh + 1) * 1024]),
                           reads=[xk, f"THF{h2}"], writes=[xk])
                        op("vector", lambda e, t=xt, hh=h2: e.reduce_max(
                            RW[:, hh:hh + 1], t[:, :], axis=AX.X, negate=True),
                           reads=[xk], writes=[f"RW{h2}"])
                    op("vector", lambda e: e.tensor_tensor(RW[:, 2:3], RW[:, 0:1], RW[:, 1:2], ALU.min),
                       reads=["RW0", "RW1"], writes=["RW2"])
                    for h2 in range(2):
                        xt = X[2 + h2]
                        xk = f"X{2 + h2}"
                        op("scalar", lambda e, t=xt, hh=h2: e.activation(
                            XLT[0][:, hh * 1024:(hh + 1) * 1024], t[:, :], AF.Exp,
                            bias=RW[:, 2:3], accum_out=RW[:, 4 + hh:5 + hh]),
                           reads=[xk, "RW2"], writes=["XLT0", f"RW{4 + h2}"])
                    op("vector", lambda e: e.tensor_add(RW[:, 6:7], RW[:, 4:5], RW[:, 5:6]),
                       reads=["RW4", "RW5"], writes=["RW6"])
                    op("vector", lambda e: e.reciprocal(RW[:, 7:8], RW[:, 6:7]),
                       reads=["RW6"], writes=["RW7"])
                    for h2 in range(2):
                        op("vector", lambda e, hh=h2: e.tensor_scalar_mul(
                            XLT[0][:, hh * 1024:(hh + 1) * 1024],
                            XLT[0][:, hh * 1024:(hh + 1) * 1024], RW[:, 7:8]),
                           reads=["XLT0", "RW7"], writes=["XLT0"])
                    for tb in range(2):
                        def tfn(e, t0=tb * 8):
                            ins = None
                            for u in range(8):
                                ins = e.transpose(psT[:, u * 128:(u + 1) * 128],
                                                  XLT[0][:, (t0 + u) * 128:(t0 + u + 1) * 128],
                                                  IDB[:, :])
                            return ins
                        op("tensor", tfn, reads=["XLT0", "IDB"], writes=["psT"])
                        def pev(e, t0=tb * 8, cc=ci):
                            ins = None
                            for u in range(8):
                                ins = e.tensor_copy(
                                    WB1[:, 4096 + (t0 + u) * 512 + cc * 128: 4096 + (t0 + u) * 512 + cc * 128 + 128],
                                    psT[:, u * 128:(u + 1) * 128])
                            return ins
                        op("vector", pev, reads=["psT"], writes=wb1r(4096, 12288))
                for dc in range(8):
                    ps = psU if dc % 2 == 0 else psV
                    pk = "psU" if dc % 2 == 0 else "psV"
                    def avmm(e, d0=dc, p=ps):
                        ins = None
                        for t_ in range(16):
                            ins = e.matmul(p[:, :],
                                           WB2[:, 16384 + t_ * 1024 + d0 * 128: 16384 + t_ * 1024 + d0 * 128 + 128],
                                           WB1[:, 4096 + t_ * 512: 4096 + (t_ + 1) * 512],
                                           start=(t_ == 0), stop=(t_ == 15))
                        return ins
                    op("tensor", avmm, reads=v_keys + wb1r(4096, 12288), writes=[pk])
                    op("vector", lambda e, d0=dc, p=ps: e.tensor_copy(
                        H[:, d0 * 512:(d0 + 1) * 512], p[:, :]),
                       reads=[pk], writes=[f"H{2 * dc}", f"H{2 * dc + 1}"])
                for ci in range(4):
                    jj = jb + g * 4 + ci
                    xi2 = ci % 2
                    op("scalar", lambda e, j0=jj, t=X[xi2]: e.dma_start(
                        out=t[:, :], in_=xbuf[j0 * 128:(j0 + 1) * 128, :]),
                       reads=[f"xbuf{jj}"], writes=[f"X{xi2}"], dma=True, queue="scalar")
                    for dh in range(2):
                        ps = psU if dh == 0 else psV
                        pk = "psU" if dh == 0 else "psV"
                        def wmm(e, cc=ci, dd=dh, p=ps):
                            ins = e.matmul(p[:, :], ONE1B[:, :], BOR[0:1, dd * 512:(dd + 1) * 512],
                                           start=True, stop=False)
                            for c in range(8):
                                ins = e.matmul(p[:, :], H[:, c * 512 + cc * 128: c * 512 + cc * 128 + 128],
                                               WB1[:, 24576 + c * 1024 + dd * 512: 24576 + c * 1024 + dd * 512 + 512],
                                               start=False, stop=(c == 7))
                            return ins
                        op("tensor", wmm,
                           reads=[f"H{u}" for u in range(16)] + ["ONE1B", "BOR"] + wb1r(24576, 32768),
                           writes=[pk])
                        op("vector", lambda e, dd=dh, p=ps, t=X[xi2]: e.scalar_tensor_tensor(
                            t[:, dd * 512:(dd + 1) * 512], p[:, :], 1.0,
                            t[:, dd * 512:(dd + 1) * 512], ALU.mult, ALU.add),
                           reads=[pk, f"X{xi2}"], writes=[f"X{xi2}"])
                    stats_tile(f"X{xi2}", X[xi2], jj)
                    op("scalar", lambda e, j0=jj, t=X[xi2]: e.dma_start(
                        out=dest[j0 * 128:(j0 + 1) * 128, :], in_=t[:, :]),
                       reads=[f"X{xi2}"], writes=[f"{destname}{jj}"], dma=True, queue="scalar")

    def store_dummy(dest, destname):
        for j in range(NT):
            op("scalar", lambda e, jj=j: e.dma_start(
                out=dest[jj * 128:(jj + 1) * 128, :], in_=xbuf[jj * 128:(jj + 1) * 128, :]),
               reads=[f"xbuf{j}"], writes=[f"{destname}{j}"], dma=True, queue="scalar")

    def phase_l1pre():
        ln_finalize()
        for sl in range(NT // 2):
            xi = (sl % 2) * 2
            load_x_slab(sl, xi)
            for k in range(2):
                j = sl * 2 + k
                op("vector", lambda e, t=X[xi + k], jj=j: e.tensor_scalar(
                    t[:, :], t[:, :], MU[:, jj:jj + 1], RSTD[:, jj:jj + 1],
                    ALU.subtract, ALU.mult),
                   reads=[f"X{xi + k}", "MU", "RSTD"], writes=[f"X{xi + k}"])
                stats_tile(f"X{xi + k}", X[xi + k], j)
            store_x_slab(sl, xi, xb_d, "xb")

    def phase_l1attn():
        ln_finalize()
        for b in range(NB if _L1P >= 2 else 0):
            jb = b * 16
            for wi, wn in enumerate(["l1wq", "l1wk", "l1wv"]):
                for c in range(8):
                    stream_w_chunk(wn, wi * 8192, c)
            for sl in range(8 if _L1A >= 2 else 0):
                xi = (sl % 2) * 2
                for k in range(2):
                    j = jb + sl * 2 + k
                    op("scalar", lambda e, jj=j, t=X[xi + k]: e.dma_start(
                        out=t[:, :], in_=xb_d[jj * 128:(jj + 1) * 128, :]),
                       reads=[f"xb{j}"], writes=[f"X{xi + k}"], dma=True, queue="scalar")
                    ln_apply(f"X{xi + k}", X[xi + k], j, "XLN0", XLN[0])
                    transpose_tile("XLN0", XLN[0], "XLT0", XLT[0],
                                   (lambda c: c * 256), k * 128)
                for ec in range(8 if _L1A >= 3 else 0):
                    ps = psU if ec % 2 == 0 else psV
                    pk = "psU" if ec % 2 == 0 else "psV"
                    def qmm(e, ee=ec, p=ps):
                        ins = None
                        for c in range(8):
                            ins = e.matmul(p[:, 0:256],
                                           WB1[:, c * 1024 + ee * 128: c * 1024 + ee * 128 + 128],
                                           XLT[0][:, c * 256:(c + 1) * 256],
                                           start=(c == 0), stop=(c == 7))
                        return ins
                    op("tensor", qmm, reads=["XLT0"] + wb1r(0, 8192), writes=[pk])
                    op("scalar", lambda e, ee=ec, p=ps: e.activation(
                        THF[:, 0:256], p[:, 0:256], AF.Relu, bias=BQE[:, ee:ee + 1]),
                       reads=[pk, "BQE", "THF0"], writes=["THF0"])
                    op("vector", lambda e, ee=ec, p=ps: e.tensor_scalar(
                        THF[:, 1024:1280], p[:, 0:256], BQE[:, ee:ee + 1], 0.0,
                        ALU.add, ALU.min),
                       reads=[pk, "BQE", "THF1"], writes=["THF1"])
                    op("scalar", lambda e: e.activation(THF[:, 1024:1280], THF[:, 1024:1280], AF.Exp),
                       reads=["THF1"], writes=["THF1"])
                    op("vector", lambda e, ee=ec: e.tensor_add(
                        H[:, ee * 256:(ee + 1) * 256], THF[:, 0:256], THF[:, 1024:1280]),
                       reads=["THF0", "THF1"], writes=[f"H{ec}"])
                if _L1A >= 3:
                    def qst(e, ss=sl):
                        return [e.dma_start(out=s_qT[ec * 128:(ec + 1) * 128, ss * 256:(ss + 1) * 256],
                                            in_=H[:, ec * 256:(ec + 1) * 256])
                                for ec in range(8)]
                    op("sync", qst, reads=[f"H{ec}" for ec in range(8)],
                       writes=[f"sqkT0_{sl}"], dma=True, n_dma=8)
                for wv_, (w0, seedr, seedk, cbase, do_elu) in (enumerate(
                        ((8192, BKER, "BKER", 16384, True), (16384, BVER, "BVER", 0, False)))
                        if _L1A >= 4 else []):
                    for k in range(2):
                        tt = sl * 2 + k
                        for dh in range(2):
                            ps = psS[:, dh * 512:(dh + 1) * 512] if wv_ == 0 else psS[:, 1024 + dh * 512:1536 + dh * 512]
                            pk = f"psS_b{dh}" if wv_ == 0 else f"psS_b{2 + dh}"
                            def kvm(e, kk=k, dd=dh, p=ps, ww=w0, sr=seedr):
                                ins = e.matmul(p, ONE1B[:, :], sr[0:1, dd * 512:(dd + 1) * 512],
                                               start=True, stop=False)
                                for c in range(8):
                                    ins = e.matmul(p, XLT[0][:, c * 256 + kk * 128: c * 256 + kk * 128 + 128],
                                                   WB1[:, ww + c * 1024 + dd * 512: ww + c * 1024 + dd * 512 + 512],
                                                   start=False, stop=(c == 7))
                                return ins
                            op("tensor", kvm, reads=["XLT0", "ONE1B", seedk] + wb1r(w0, w0 + 8192),
                               writes=[pk])
                            dstc = cbase + tt * 1024 + dh * 512
                            if do_elu:
                                op("scalar", lambda e, p=ps: e.activation(
                                    THF[:, 0:512], p, AF.Relu),
                                   reads=[pk, "THF0"], writes=["THF0"])
                                op("vector", lambda e, p=ps: e.tensor_scalar(
                                    THF[:, 1024:1536], p, 0.0, None, ALU.min),
                                   reads=[pk, "THF1"], writes=["THF1"])
                                op("scalar", lambda e: e.activation(
                                    THF[:, 1024:1536], THF[:, 1024:1536], AF.Exp),
                                   reads=["THF1"], writes=["THF1"])
                                op("vector", lambda e, d0=dstc: e.tensor_add(
                                    WB2[:, d0:d0 + 512], THF[:, 0:512], THF[:, 1024:1536]),
                                   reads=["THF0", "THF1"], writes=wb2r(dstc, dstc + 512))
                            else:
                                op("vector", lambda e, d0=dstc, p=ps: e.tensor_copy(
                                    WB2[:, d0:d0 + 512], p),
                                   reads=[pk], writes=wb2r(dstc, dstc + 512))
            kv_keys = wb2r(0, 32768)
            for h in range(8 if _L1P >= 3 else 0):
                for t_ in range(16):
                    op("vector", lambda e, hh=h, tt=t_: e.tensor_copy(
                        XLN[0][:, 0:128], WB2[:, tt * 1024 + hh * 128: tt * 1024 + (hh + 1) * 128]),
                       reads=kv_keys + ["XLN0"], writes=["XLN0"])
                    if t_ == 0 and h == 0:
                        op("vector", lambda e: e.memset(XLN[0][:, 128:129], 1.0),
                           reads=["XLN0"], writes=["XLN0x"])
                    op("tensor", lambda e, hh=h, tt=t_: e.matmul(
                        psW[:, 0:129],
                        WB2[:, 16384 + tt * 1024 + hh * 128: 16384 + tt * 1024 + (hh + 1) * 128],
                        XLN[0][:, 0:129],
                        start=(tt == 0), stop=(tt == 15)),
                       reads=kv_keys + ["XLN0", "XLN0x"] + (["psW"] if t_ > 0 else []),
                       writes=["psW"])
                op("vector", lambda e, hh=h: e.tensor_copy(
                    KVSB[:, hh * 129: hh * 129 + 128], psW[:, 0:128]),
                   reads=["psW"], writes=["KVSB"])
                op("vector", lambda e, hh=h: e.tensor_copy(
                    KSUM[:, hh:hh + 1], psW[:, 128:129]),
                   reads=["psW"], writes=["KSUM"])
            for g in range(4 if _L1P >= 4 else 0):
                for hs in range(2):
                    def qld(e, gg=g, h0=hs * 4):
                        return [e.dma_start(out=XLT[0][:, c * 512:(c + 1) * 512],
                                            in_=s_qT[(h0 + c) * 128:(h0 + c + 1) * 128,
                                                     gg * 512:(gg + 1) * 512])
                                for c in range(4)]
                    op("sync", qld, reads=[f"sqkT0_{ss}" for ss in range(8)] + ["XLT0"],
                       writes=["XLT0"], dma=True, n_dma=4)
                    for c in range(4):
                        h = hs * 4 + c
                        op("tensor", lambda e, cc=c, hh=h: e.matmul(
                            psU[0:1, :], KSUM[:, hh:hh + 1],
                            XLT[0][:, cc * 512:(cc + 1) * 512],
                            start=True, stop=True),
                           reads=["KSUM", "XLT0"], writes=["psU"])
                        op("vector", lambda e, cc=c: e.tensor_copy(
                            THF[0:1, cc * 512:(cc + 1) * 512], psU[0:1, :]),
                           reads=["psU"], writes=["THF0", "THF1"])
                    def dtr(e, h0=hs * 4):
                        ins = None
                        for u in range(4):
                            for cc in range(4):
                                ins = e.transpose(psV[:, u * 8 + h0 + cc: u * 8 + h0 + cc + 1],
                                                  THF[0:1, cc * 512 + u * 128: cc * 512 + (u + 1) * 128],
                                                  IDF[0:1, 0:1])
                        return ins
                    op("tensor", dtr, reads=["THF0", "THF1", "IDF"], writes=["psV"])
                    op("vector", lambda e: e.tensor_scalar(ZT[:, :], psV[:, 0:32], EPS_ATTN, None, ALU.add),
                       reads=["psV"], writes=["ZTraw", "ZT"])
                    op("vector", lambda e: e.reciprocal(ZT[:, :], ZT[:, :]),
                       reads=["ZTraw"], writes=["ZT"])
                    for u in range(4):
                        j = jb + g * 4 + u
                        if hs == 0:
                            op("scalar", lambda e, jj=j, t=X[u]: e.dma_start(
                                out=t[:, :], in_=xb_d[jj * 128:(jj + 1) * 128, :]),
                               reads=[f"xb{j}"], writes=[f"X{u}"], dma=True, queue="scalar")
                        for c in range(4):
                            h = hs * 4 + c
                            bank = c % 2
                            pk = f"psS_b{bank}"
                            op("tensor", lambda e, hh=h, cc=c, uu=u, bb=bank: e.matmul(
                                psS[:, bb * 512 + (cc // 2) * 128: bb * 512 + (cc // 2) * 128 + 128],
                                XLT[0][:, cc * 512 + uu * 128: cc * 512 + uu * 128 + 128],
                                KVSB[:, hh * 129: hh * 129 + 128],
                                start=True, stop=True),
                               reads=["XLT0", "KVSB"], writes=[pk])
                            op("vector", lambda e, hh=h, cc=c, uu=u, bb=bank, t=X[u]: e.scalar_tensor_tensor(
                                t[:, hh * 128:(hh + 1) * 128],
                                psS[:, bb * 512 + (cc // 2) * 128: bb * 512 + (cc // 2) * 128 + 128],
                                ZT[:, uu * 8 + hh: uu * 8 + hh + 1],
                                t[:, hh * 128:(hh + 1) * 128], ALU.mult, ALU.add),
                               reads=[pk, "ZT", f"X{u}"], writes=[f"X{u}"])
                        if hs == 1:
                            stats_tile(f"X{u}", X[u], j)
                            op("scalar", lambda e, jj=j, t=X[u]: e.dma_start(
                                out=xb_d[jj * 128:(jj + 1) * 128, :], in_=t[:, :]),
                               reads=[f"X{u}"], writes=[f"xb{j}"], dma=True, queue="scalar")

    def phase_moe(dest, destname, ci_, co_, callid):
        ln_finalize()
        op("vector", lambda e: e.memset(RUNOFF[:, :], 0.0), writes=["RUNOFF"])
        for sl in range(NT // 2):
            xi = (sl % 2) * 2
            load_x_slab(sl, xi)
            for k in range(2):
                j = sl * 2 + k
                ln_apply(f"X{xi + k}", X[xi + k], j, "XLN0", XLN[0])
                op("sync", lambda e, jj=j: e.dma_start(
                    out=xln_d[jj * 128:(jj + 1) * 128, :], in_=XLN[0][:, :]),
                   reads=["XLN0"], writes=[f"xln{j}"], dma=True)
                transpose_tile("XLN0", XLN[0], "XLT0", XLT[0], (lambda c: c * 256), k * 128)
            for k in range(2):
                j = sl * 2 + k
                def lmm(e, kk=k):
                    ins = None
                    for c in range(8):
                        ins = e.matmul(psW[:, 0:8],
                                       XLT[0][:, c * 256 + kk * 128: c * 256 + kk * 128 + 128],
                                       WGT[:, c * 8:(c + 1) * 8],
                                       start=(c == 0), stop=(c == 7))
                    return ins
                op("tensor", lmm, reads=["XLT0", "WGT"], writes=["psW"])
                op("vector", lambda e: e.tensor_copy(LG[:, :], psW[:, 0:8]),
                   reads=["psW"], writes=["LG"])
                op("vector", lambda e: e.reduce_max(RW[:, 0:1], LG[:, :], axis=AX.X),
                   reads=["LG"], writes=["RW0"])
                op("vector", lambda e: e.reduce_max(RW[:, 1:2], LG[:, :], axis=AX.X, negate=True),
                   reads=["LG"], writes=["RW1"])
                op("scalar", lambda e, jj=j: e.activation(TMP8[:, :], LG[:, :], AF.Exp,
                                                          bias=RW[:, 1:2], accum_out=RW[:, 2:3]),
                   reads=["LG", "RW1", "TMP8"], writes=["TMP8", "RW2"])
                op("vector", lambda e, jj=j: e.reciprocal(COMB[:, jj:jj + 1], RW[:, 2:3]),
                   reads=["RW2"], writes=["COMB"])
                op("vector", lambda e: e.tensor_single_scalar(TMP8[:, :], LG[:, :], RW[:, 0:1], ALU.is_equal),
                   reads=["LG", "RW0"], writes=["TMP8"])
                op("vector", lambda e: e.tensor_mul(TMP8[:, :], TMP8[:, :], REV8[:, :]),
                   reads=["TMP8", "REV8"], writes=["TMP8"])
                op("vector", lambda e: e.reduce_max(RW[:, 3:4], TMP8[:, :], axis=AX.X),
                   reads=["TMP8"], writes=["RW3"])
                op("vector", lambda e, jj=j: e.tensor_single_scalar(
                    OHA[:, jj * 8:(jj + 1) * 8], TMP8[:, :], RW[:, 3:4], ALU.is_equal),
                   reads=["TMP8", "RW3"], writes=["OHA"])
        for j in range(NT):
            op("vector", lambda e, jj=j: e.tensor_copy(OHB[:, :], OHA[:, jj * 8:(jj + 1) * 8]),
               reads=["OHA"], writes=["OHB"])
            op("tensor", lambda e: e.matmul(psV[:, 0:8], TSTR[:, :], OHB[:, :],
                                            start=True, stop=False),
               reads=["TSTR", "OHB"], writes=["psV"])
            op("tensor", lambda e: e.matmul(psV[:, 0:8], ONE1F[:, :], RUNOFF[0:1, :],
                                            start=False, stop=True),
               reads=["ONE1F", "RUNOFF"], writes=["psV"])
            op("vector", lambda e, jj=j: e.tensor_mul(TMP8[:, :], OHA[:, jj * 8:(jj + 1) * 8], psV[:, 0:8]),
               reads=["OHA", "psV"], writes=["TMP8"])
            op("vector", lambda e, jj=j: e.reduce_sum(RANKA[:, jj:jj + 1], TMP8[:, :], axis=AX.X),
               reads=["TMP8"], writes=["RANKA"])
            op("tensor", lambda e: e.matmul(psU[0:1, 0:8], ONEC_B[:, :], OHB[:, :],
                                            start=True, stop=True),
               reads=["ONEC_B", "OHB"], writes=["psU"])
            op("vector", lambda e: e.tensor_add(RUNOFF[0:1, :], RUNOFF[0:1, :], psU[0:1, 0:8]),
               reads=["RUNOFF", "psU"], writes=["RUNOFF"])
        op("gpsimd", lambda e: e.dma_start(out=ci_[:, :], in_=ZB[:, :]),
           reads=["ZB"], writes=["ccin"], dma=True)
        op("gpsimd", lambda e: e.dma_start(
            out=ci_[bass.ts(e.partition_id(), 1), 0:8], in_=RUNOFF[0:1, :]),
           reads=["RUNOFF", "ccin"], writes=["ccin"], dma=True)
        op("gpsimd", lambda e: e.collective_compute(
            "AllReduce", ALU.add, replica_groups=[list(range(NCORE))],
            ins=[ci_[:, :].opt()], outs=[co_[:, :].opt()]),
           reads=["ccin"], writes=["ccout"], cc=True)
        op("gpsimd", lambda e: e.dma_start(out=MROW[:, :], in_=co_[0:8, 0:8]),
           reads=["ccout"], writes=["MROW"], dma=True)
        op("tensor", lambda e: e.matmul(psU[0:1, 0:8], MKC[0:8, 0:1], MROW[:, :],
                                        start=True, stop=True),
           reads=["MKC", "MROW"], writes=["psU"])
        op("vector", lambda e: e.tensor_copy(ROW8[:, :], psU[0:1, 0:8]),
           reads=["psU"], writes=["ROW8"])
        op("vector", lambda e: e.tensor_scalar(NV8[:, :], ROW8[:, :], -1.0, float(CAP),
                                               ALU.mult, ALU.add),
           reads=["ROW8"], writes=["NV8"])
        op("vector", lambda e: e.tensor_scalar(NV8[:, :], NV8[:, :], 0.0, None, ALU.max),
           reads=["NV8"], writes=["NV8"])
        op("vector", lambda e: e.tensor_tensor(NV8[:, :], NV8[:, :], RUNOFF[0:1, :], ALU.min),
           reads=["NV8", "RUNOFF"], writes=["NV8"])
        op("vector", lambda e: e.tensor_scalar(TMP8[0:1, 0:8], NV8[:, :], 1.0 / 512.0,
                                               511.0 / 512.0 - 0.49999, ALU.mult, ALU.add),
           reads=["NV8"], writes=["TMP8"])
        op("vector", lambda e: e.tensor_copy(I8[:, :], TMP8[0:1, 0:8]), reads=["TMP8"], writes=["I8"])
        op("vector", lambda e: e.tensor_copy(TMP8[0:1, 0:8], I8[:, :]), reads=["I8"], writes=["TMP8"])
        op("vector", lambda e: e.tensor_scalar(TMP8[0:1, 0:8], TMP8[0:1, 0:8], 512.0, None, ALU.mult),
           reads=["TMP8"], writes=["TMP8"])
        op("tensor", lambda e: e.transpose(psW[0:8, 0:1], TMP8[0:1, 0:8], IDF[0:1, 0:1]),
           reads=["TMP8", "IDF"], writes=["psW"])
        op("vector", lambda e: e.tensor_copy(PCOL[:, :], psW[0:8, 0:1]), reads=["psW"], writes=["PCOL"])
        op("tensor", lambda e: e.matmul(psU[0:8, 0:1], TRI8[:, :], PCOL[:, :], start=True, stop=True),
           reads=["TRI8", "PCOL"], writes=["psU"])
        op("vector", lambda e: e.tensor_copy(PCOL[:, :], psU[0:8, 0:1]), reads=["psU"], writes=["PCOL"])
        op("tensor", lambda e: e.transpose(psW[0:1, 0:8], PCOL[0:8, 0:1], IDF[0:8, 0:8]),
           reads=["PCOL", "IDF"], writes=["psW"])
        op("vector", lambda e: e.tensor_copy(SS8[:, :], psW[0:1, 0:8]), reads=["psW"], writes=["SS8"])
        for src, srk, dstt, dstk in ((NV8, "NV8", NVB, "NVB"), (SS8, "SS8", SEGB, "SEGB"),
                                     (ROW8, "ROW8", OFSB, "OFSB")):
            op("tensor", lambda e, s0=src: e.matmul(psW[:, 0:8], ONE1F[:, :], s0[0:1, :],
                                                    start=True, stop=True),
               reads=[srk, "ONE1F"], writes=["psW"])
            op("vector", lambda e, d0=dstt: e.tensor_copy(d0[:, :], psW[:, 0:8]),
               reads=["psW"], writes=[dstk])
        op("tensor", lambda e: e.matmul(psW[0:24, 8:16], ONE1F[0:1, 0:24], SS8[0:1, :],
                                        start=True, stop=True),
           reads=["SS8", "ONE1F"], writes=["psW"])
        op("vector", lambda e: e.tensor_copy(SEGS[:, :], psW[0:24, 8:16]), reads=["psW"], writes=["SEGS"])
        op("vector", lambda e: e.tensor_single_scalar(SEGS[:, :], SEGS[:, :], GS24T[:, 0:1], ALU.is_le),
           reads=["SEGS", "GS24T"], writes=["SEGS"])
        op("vector", lambda e: e.reduce_sum(GS24T[:, 0:1] if False else SEGS[:, 0:1], SEGS[:, :], axis=AX.X),
           reads=["SEGS"], writes=["SEGS"])
        op("vector", lambda e: e.tensor_scalar(SEGS[:, 0:1], SEGS[:, 0:1], -1.0, None, ALU.add),
           reads=["SEGS"], writes=["SEGS"])
        op("tensor", lambda e: e.transpose(psW[0:1, 16:40], SEGS[0:24, 0:1], IDF[0:24, 0:24]),
           reads=["SEGS", "IDF"], writes=["psW"])
        op("vector", lambda e: e.tensor_copy(ETABF[:, :], psW[0:1, 16:40]),
           reads=["psW"], writes=["ETABF"])
        op("tensor", lambda e: e.matmul(psW[:, 40:64], ONE1F[:, :], ETABF[0:1, :],
                                        start=True, stop=True),
           reads=["ETABF", "ONE1F"], writes=["psW"])
        op("vector", lambda e: e.tensor_copy(EGB[:, :], psW[:, 40:64]),
           reads=["psW"], writes=["EGB"])
        op("vector", lambda e: e.memset(TM96[:, :], 0), writes=["TM96"])
        op("gpsimd", lambda e: e.dma_start(
            out=tokmap_d.rearrange("(p c) x -> p (c x)", p=128), in_=TM96[:, :]),
           reads=["TM96"], writes=["tmz"], dma=True)
        allt = [f"tm{j}" for j in range(NT)]
        for j in range(NT):
            oh = OHA[:, j * 8:(j + 1) * 8]
            op("vector", lambda e, o=oh: e.tensor_mul(TMP8[:, :], o, SEGB[:, :]),
               reads=["OHA", "SEGB"], writes=["TMP8"])
            op("vector", lambda e: e.reduce_sum(RW[:, 0:1], TMP8[:, :], axis=AX.X),
               reads=["TMP8"], writes=["RW0"])
            op("vector", lambda e, o=oh: e.tensor_mul(TMP8[:, :], o, NVB[:, :]),
               reads=["OHA", "NVB"], writes=["TMP8"])
            op("vector", lambda e: e.reduce_sum(RW[:, 1:2], TMP8[:, :], axis=AX.X),
               reads=["TMP8"], writes=["RW1"])
            op("vector", lambda e, jj=j: e.tensor_single_scalar(RW[:, 2:3], RANKA[:, jj:jj + 1],
                                                                RW[:, 1:2], ALU.is_lt),
               reads=["RANKA", "RW1"], writes=["RW2"])
            op("vector", lambda e, jj=j: e.tensor_add(RW[:, 3:4], RANKA[:, jj:jj + 1], RW[:, 0:1]),
               reads=["RANKA", "RW0"], writes=["RW3"])
            op("vector", lambda e: e.tensor_scalar(RW[:, 3:4], RW[:, 3:4], -1048576.0, None, ALU.add),
               reads=["RW3"], writes=["RW3"])
            op("vector", lambda e: e.tensor_mul(RW[:, 3:4], RW[:, 3:4], RW[:, 2:3]),
               reads=["RW3", "RW2"], writes=["RW3"])
            op("vector", lambda e: e.tensor_scalar(RW[:, 3:4], RW[:, 3:4], 1048576.0, None, ALU.add),
               reads=["RW3"], writes=["RW3"])
            op("vector", lambda e: e.tensor_scalar(RW[:, 3:4], RW[:, 3:4], float(24 * 512 - 1), None, ALU.min),
               reads=["RW3"], writes=["RW3"])
            op("vector", lambda e, jj=j: e.tensor_copy(SLOTI[:, jj:jj + 1], RW[:, 3:4]),
               reads=["RW3"], writes=["SLOTI"])
            op("vector", lambda e, jj=j: e.tensor_mul(RW[:, 4:5], COMB[:, jj:jj + 1], RW[:, 2:3]),
               reads=["COMB", "RW2"], writes=["RW4"])
            op("vector", lambda e, jj=j: e.tensor_scalar(COEF[:, jj:jj + 1], RW[:, 4:5], 0.5, None, ALU.mult),
               reads=["RW4"], writes=["COEF"])
            op("vector", lambda e, jj=j: e.tensor_scalar(RW[:, 5:6], IOTA128[:, :], 1.0, float(jj * 128),
                                                         ALU.mult, ALU.add),
               reads=["IOTA128"], writes=["RW5"])
            op("vector", lambda e: e.tensor_copy(TMAPV[:, 0:1], RW[:, 5:6]),
               reads=["RW5", "TMAPV"], writes=["TMAPV"])
            op("gpsimd", lambda e, jj=j: e.indirect_dma_start(
                out=tokmap_d[:, :], out_offset=IOA(SLOTI[:, jj:jj + 1], 0),
                in_=TMAPV[:, :], in_offset=None),
               reads=["TMAPV", "SLOTI", "tmz"], writes=[f"tm{j}"], dma=True)
        allxln = [f"xln{j}" for j in range(NT)]
        for g in range(24):
            op("vector", lambda e, gg=g: e.tensor_scalar_mul(IDXWF[:, :], MULTW[:, :], EGB[:, gg:gg + 1]),
               reads=["MULTW", "EGB"], writes=["IDXWF"])
            op("vector", lambda e: e.tensor_add(IDXWF[:, :], IDXWF[:, :], BIDXW[:, :]),
               reads=["IDXWF", "BIDXW"], writes=["IDXWF"])
            op("vector", lambda e: e.tensor_copy(IDXWI[:, :], IDXWF[:, :]),
               reads=["IDXWF"], writes=["IDXWI"])
            op("vector", lambda e, gg=g: e.tensor_scalar(RW[:, 6:7], IOTA128[:, :], 1.0, None,
                                                         ALU.mult, accum_out=None)
               if False else e.tensor_scalar(RW[:, 6:7], EGB[:, gg:gg + 1], 128.0, None, ALU.mult),
               reads=["EGB"], writes=["RW6"])
            op("vector", lambda e: e.tensor_add(RW[:, 6:7], RW[:, 6:7], IOTA128[:, :]),
               reads=["RW6", "IOTA128"], writes=["RW6"])
            op("vector", lambda e: e.tensor_copy(IDXB[:, :], RW[:, 6:7]),
               reads=["RW6"], writes=["IDXB"])
            op("vector", lambda e, gg=g: e.tensor_copy(IDXB2[:, :], EGB[0:2, gg:gg + 1]),
               reads=["EGB"], writes=["IDXB2"])
            for c in range(8):
                op("gpsimd", lambda e, cc=c: e.indirect_dma_start(
                    out=WB1[:, cc * 4096:(cc + 1) * 4096], out_offset=None,
                    in_=moew1_d[:, :], in_offset=IOA(IDXWI[:, cc:cc + 1], 0)),
                   reads=["IDXWI", "moew1_d"], writes=[f"WB1_{c}"], dma=True)
            for f in range(32):
                op("gpsimd", lambda e, ff=f: e.indirect_dma_start(
                    out=WB2[:, ff * 1024:(ff + 1) * 1024], out_offset=None,
                    in_=moew2_d[:, :], in_offset=IOA(IDXWI[:, 8 + ff:9 + ff], 0)),
                   reads=["IDXWI", "moew2_d"], writes=[f"WB2_{f // 4}"], dma=True)
            op("gpsimd", lambda e: e.indirect_dma_start(
                out=B1CUR[:, :], out_offset=None,
                in_=b1T_d[:, :], in_offset=IOA(IDXB[:, 0:1], 0)),
               reads=["IDXB", "b1T_d"], writes=["B1CUR"], dma=True)
            op("gpsimd", lambda e: e.indirect_dma_start(
                out=B2CUR[:, :], out_offset=None,
                in_=b2w_d[:, :], in_offset=IOA(IDXB2[:, 0:1], 0)),
               reads=["IDXB2", "b2w_d"], writes=["B2CUR"], dma=True)
            wb1keys = [f"WB1_{r}" for r in range(8)]
            wb2keys = [f"WB2_{r}" for r in range(8)]
            for sub in range(2):
                s0 = g * 512 + sub * 256
                for k in range(2):
                    idxt = IDX4 if k == 0 else IDX4B
                    idxk = "IDX4" if k == 0 else "IDX4B"
                    op("scalar", lambda e, ss=s0 + k * 128, it=idxt: e.dma_start(
                        out=it[:, :], in_=tokmap_d[ss:ss + 128, :]),
                       reads=allt + ["tmz"], writes=[idxk], dma=True, queue="scalar")
                    op("gpsimd", lambda e, kk=k, it=idxt: e.indirect_dma_start(
                        out=EB[kk][:, :], out_offset=None,
                        in_=xln_d[:, :], in_offset=IOA(it[:, 0:1], 0)),
                       reads=allxln + [idxk, f"EB{k}"], writes=[f"EB{k}"], dma=True)
                    transpose_tile(f"EB{k}", EB[k], "XLT0", XLT[0], (lambda c: c * 256), k * 128)
                outps = {(0, 0): psS[:, 1024:1536], (0, 1): psS[:, 1536:2048],
                         (1, 0): psU[:, :], (1, 1): psV[:, :]}
                outpk = {(0, 0): "psS_b2", (0, 1): "psS_b3", (1, 0): "psU", (1, 1): "psV"}
                for half in range(2):
                    for fi in range(16):
                        f = half * 16 + fi
                        hp3 = (half * 16 + fi) % 3
                        ps = (psS[:, 0:256], psS[:, 512:768], psW[:, 0:256])[hp3]
                        pk = ("psS_b0", "psS_b1", "psW")[hp3]
                        def hmm(e, ff=f, p=ps):
                            ins = None
                            for c in range(8):
                                ins = e.matmul(p, WB1[:, c * 4096 + ff * 128: c * 4096 + ff * 128 + 128],
                                               XLT[0][:, c * 256:(c + 1) * 256],
                                               start=(c == 0), stop=(c == 7))
                            return ins
                        op("tensor", hmm, reads=["XLT0"] + wb1keys, writes=[pk])
                        op("scalar", lambda e, ff=f, fj=fi, p=ps: e.activation(
                            H[:, fj * 256:(fj + 1) * 256], p, AF.Gelu, bias=B1CUR[:, ff:ff + 1]),
                           reads=[pk, "B1CUR"], writes=[f"H{fi}"])
                    hkeys = [f"H{fi}" for fi in range(16)]
                    for k in range(2):
                        for dh in range(2):
                            ps = outps[(k, dh)]
                            pk = outpk[(k, dh)]
                            def omm(e, kk=k, dd=dh, p=ps, hh=half):
                                ins = None
                                if hh == 0:
                                    ins = e.matmul(p, ONE1B[:, :], B2CUR[0:1, dd * 512:(dd + 1) * 512],
                                                   start=True, stop=False)
                                for fi in range(16):
                                    f = hh * 16 + fi
                                    ins = e.matmul(p, H[:, fi * 256 + kk * 128: fi * 256 + kk * 128 + 128],
                                                   WB2[:, f * 1024 + dd * 512: f * 1024 + dd * 512 + 512],
                                                   start=False, stop=(hh == 1 and fi == 15))
                                return ins
                            rds = hkeys + wb2keys + ["ONE1B", "B2CUR"]
                            if half == 1:
                                rds = rds + [pk]
                            op("tensor", omm, reads=rds, writes=[pk])
                for k in range(2):
                    for dh in range(2):
                        ps = outps[(k, dh)]
                        pk = outpk[(k, dh)]
                        op("vector", lambda e, kk=k, dd=dh, p=ps: e.tensor_copy(
                            THF[:, kk * 1024 + dd * 512: kk * 1024 + (dd + 1) * 512], p),
                           reads=[pk], writes=[f"THF{k}"])
                    op("sync", lambda e, ss=s0 + k * 128, kk=k: e.dma_start(
                        out=ysort_d[ss:ss + 128, :], in_=THF[:, kk * 1024:(kk + 1) * 1024]),
                       reads=[f"THF{k}"], writes=[f"ys{g}"], dma=True)
        allys = [f"ys{g}" for g in range(24)]
        for j in range(NT):
            xi = j % 2
            op("scalar", lambda e, jj=j, t=X[xi]: e.dma_start(
                out=t[:, :], in_=xbuf[jj * 128:(jj + 1) * 128, :]),
               reads=[f"xbuf{j}"], writes=[f"X{xi}"], dma=True, queue="scalar")
            yk = "THF0" if xi == 0 else "THF1"
            yt = THF[:, 0:1024] if xi == 0 else THF[:, 1024:2048]
            op("gpsimd", lambda e, jj=j, y0=yt: e.indirect_dma_start(
                out=y0, out_offset=None,
                in_=ysort_d[:, :], in_offset=IOA(SLOTI[:, jj:jj + 1], 0)),
               reads=allys + ["SLOTI", yk], writes=[yk], dma=True)
            op("vector", lambda e, jj=j, t=X[xi], y0=yt: e.scalar_tensor_tensor(
                t[:, :], y0, COEF[:, jj:jj + 1], t[:, :], ALU.mult, ALU.add),
               reads=[yk, "COEF", f"X{xi}"], writes=[f"X{xi}"])
            stats_tile(f"X{xi}", X[xi], j)
            op("scalar", lambda e, jj=j, t=X[xi]: e.dma_start(
                out=dest[jj * 128:(jj + 1) * 128, :], in_=t[:, :]),
               reads=[f"X{xi}"], writes=[f"{destname}{j}"], dma=True, queue="scalar")

    load_consts()
    if _STAGES >= 1:
        prep_l0()
        prep_attn_l0()
    if _STAGES >= 2:
        phase_load_x()
    if _STAGES >= 1:
        if _STAGES >= 6:
            prep_moe()
        if _STAGES >= 7 and _L1P >= 1:
            prep_l1()
    phases = []
    if _STAGES >= 3:
        phases.append(phase_l0ffn)
    if _STAGES >= 4:
        phases.append(phase_l0attn)
    if _STAGES >= 5:
        phases.append(phase_l0ffn)
    if _STAGES >= 6:
        phases.append(lambda d, n: phase_moe(d, n, cc_in, cc_out, 0))
    if _STAGES >= 7:
        if _L1P >= 5:
            phases.append(lambda d, n: (phase_l1pre(), phase_l1attn(), phase_l1ffn(d, n)))
        elif _L1P >= 1:
            phases.append(lambda d, n: (phase_l1pre(), phase_l1attn(), store_dummy(d, n)))
        else:
            phases.append(lambda d, n: store_dummy(d, n))
    if _STAGES >= 8:
        phases.append(lambda d, n: phase_moe(d, n, cc_in2, cc_out2, 1))
    for pi, ph in enumerate(phases):
        last = (pi == len(phases) - 1)
        ph(y_out if last else xbuf, "y" if last else "xbuf")

    with nc.Block() as block:
        @block.sync
        def _(e):
            em.replay("sync", e)

        @block.scalar
        def _(e):
            em.replay("scalar", e)

        @block.vector
        def _(e):
            em.replay("vector", e)

        @block.tensor
        def _(e):
            em.replay("tensor", e)

        @block.gpsimd
        def _(e):
            em.replay("gpsimd", e)
            em.finish("gpsimd", e)

    return nc


WSHAPES = {
    "L0_attn_Wq": (D, D), "L0_attn_bq": (D,), "L0_attn_Wk": (D, D), "L0_attn_bk": (D,),
    "L0_attn_Wv": (D, D), "L0_attn_bv": (D,), "L0_attn_Wo": (D, D), "L0_attn_bo": (D,),
    "L0_ffn_W1": (4096, D), "L0_ffn_b1": (4096,), "L0_ffn_W2": (D, 4096), "L0_ffn_b2": (D,),
    "L1_attn_wq": (D, D), "L1_attn_wk": (D, D), "L1_attn_wv": (D, D),
    "L1_attn_w1": (4096, D), "L1_attn_b1": (4096,), "L1_attn_w2": (D, 4096), "L1_attn_b2": (D,),
    "L1_attn_g1": (D,), "L1_attn_be1": (D,), "L1_attn_g2": (D,), "L1_attn_be2": (D,),
    "L1_ffn_wg": (D, 8), "L1_ffn_W1": (8, D, 4096), "L1_ffn_b1": (8, 4096),
    "L1_ffn_W2": (8, 4096, D), "L1_ffn_b2": (8, D),
}


def kernel(**inputs):
    x_full = np.ascontiguousarray(inputs["L0_attn_x"].reshape(32 * S, D))
    shards = [x_full[c * T:(c + 1) * T] for c in range(NCORE)]
    wmap = {}
    for name, shape in WSHAPES.items():
        arr = np.ascontiguousarray(inputs[name])
        if arr.ndim > 2 or name in ("L1_ffn_W1", "L1_ffn_W2", "L1_ffn_b1", "L1_ffn_b2"):
            arr = arr.reshape(arr.shape[0] * arr.shape[1], -1) if arr.ndim == 3 else arr
        wmap[name] = arr
    shapes = {k: v.shape for k, v in wmap.items()}
    nc = build(shapes)
    in_maps = []
    for c in range(NCORE):
        m = {"x": shards[c]}
        m.update(wmap)
        in_maps.append(m)
    res = run_bass_kernel_spmd(nc, in_maps, core_ids=list(range(NCORE)))
    out = np.concatenate([res.results[c]["y"] for c in range(NCORE)], axis=0)
    return out.reshape(32, S, D)


# version 05
# speedup vs baseline: 1.0073x; 1.0073x over previous
import numpy as np
import ml_dtypes
import re as _re
import concourse.bass as bass
from concourse.bass_primitives_rust import RegisterHandle as _RegisterHandle
import concourse.mybir as mybir
from concourse.bass_utils import run_bass_kernel_spmd

F32 = mybir.dt.float32
BF16 = mybir.dt.bfloat16
I32 = mybir.dt.int32
AF = mybir.ActivationFunctionType
ALU = mybir.AluOpType
AX = mybir.AxisListType
IOA = bass.IndirectOffsetOnAxis

D = 1024
NB = 4
S = 2048
T = NB * S
NT = T // 128
CAP = int(8 * T / 8 * 1.25) + 1
EPS_LN = 1e-5
EPS_ATTN = 1e-6
NCORE = 8

ENGS = ("tensor", "vector", "scalar", "gpsimd", "sync")


class Em:

    DMA_ROT = 12

    def __init__(self, nc, paranoid=False):
        self.paranoid = paranoid
        self.nc = nc
        self.ops = {e: [] for e in ENGS}
        self.sems = {}
        self.counts = {}
        self.floors = {e: {} for e in ENGS}
        self.lastw = {}
        self.readers = {}
        for e in ENGS:
            self._mk("p_" + e)
        self.dmarot = {}
        for q in ("sync", "scalar", "gpsimd", "vector", "cc"):
            for i in range(self.DMA_ROT):
                self._mk(f"d_{q}_{i}")
            self.dmarot[q] = 0

    def _mk(self, name):
        self.sems[name] = self.nc.alloc_semaphore(name)
        self.counts[name] = 0

    def op(self, eng, fn, reads=(), writes=(), dma=False, queue=None, cc=False, n_dma=1, free_regs=False):
        waits = {}

        def need(s, v):
            if v > waits.get(s, -1):
                waits[s] = v

        if cc or dma:
            q = "cc" if cc else (queue or eng)
            i = self.dmarot[q]
            self.dmarot[q] = (i + 1) % self.DMA_ROT
            semname = f"d_{q}_{i}"
            inc = 1 if cc else 16 * n_dma
            if self.counts[semname] > 0:
                need(semname, self.counts[semname])
        else:
            semname = "p_" + eng
            inc = 1

        for k in reads:
            for sn, v in self.lastw.get(k, {}).items():
                need(sn, v)
            if k.startswith("ps"):
                for sn, v in self.readers.get(k, {}).items():
                    need(sn, v)
        for k in writes:
            for sn, v in self.lastw.get(k, {}).items():
                need(sn, v)
            for sn, v in self.readers.get(k, {}).items():
                need(sn, v)
        wl = []
        fl = self.floors[eng]
        for s, v in sorted(waits.items()):
            if self.paranoid or v > fl.get(s, 0):
                wl.append((s, v))
                fl[s] = max(v, fl.get(s, 0))
        self.counts[semname] += inc
        newv = self.counts[semname]
        self.ops[eng].append((wl, fn, semname, inc, free_regs))
        rec = dict(waits)
        rec[semname] = newv
        for k in writes:
            self.lastw[k] = dict(rec)
            self.readers[k] = {}
        for k in reads:
            r = self.readers.setdefault(k, {})
            for sn, v in rec.items():
                if v > r.get(sn, -1):
                    r[sn] = v
        return (semname, newv)

    def barrier_key(self, key):
        return key

    ENG_TYPE = {"gpsimd": mybir.EngineType.Pool, "vector": mybir.EngineType.DVE,
                "scalar": mybir.EngineType.Activation, "tensor": mybir.EngineType.PE,
                "sync": mybir.EngineType.SP}

    def replay(self, eng, e):
        et = self.ENG_TYPE[eng]
        for wl, fn, semname, inc, free_regs in self.ops[eng]:
            for s, v in wl:
                e.wait_ge(self.sems[s], v)
            ins = fn(e)
            insl = ins if isinstance(ins, list) else ([] if ins is None else [ins])
            if isinstance(ins, list):
                for i2 in insl:
                    i2.then_inc(self.sems[semname], 16)
            elif ins is not None:
                ins.then_inc(self.sems[semname], inc)
            if free_regs:
                for i2 in insl:
                    for nm in set(_re.findall(r"R\[([A-Za-z_0-9]+)\]", str(i2.ins))):
                        e.free_register(_RegisterHandle(nm, et))

    def finish(self, eng, e):
        for s, c in self.counts.items():
            if c > 0:
                e.wait_ge(self.sems[s], c)


_PARANOID = False
_STAGES = 99
_SUB = 99
_L1P = 99
_L1A = 99


def build(core_weight_names):
    nc = bass.Bass("TRN2", target_bir_lowering=False, dynamic_dma_scratch_size=16384)
    em = Em(nc, paranoid=_PARANOID)

    x_in = nc.dram_tensor("x", [T, D], F32, kind="ExternalInput").ap()
    y_out = nc.dram_tensor("y", [T, D], F32, kind="ExternalOutput").ap()
    W = {}
    for name, shape in core_weight_names.items():
        W[name] = nc.dram_tensor(name, list(shape), F32, kind="ExternalInput").ap()

    xbuf = nc.dram_tensor("xbuf", [T, D], F32, kind="Internal").ap()
    stg = {
        "l0w1": nc.dram_tensor("stg_l0w1", [4096, D], BF16, kind="Internal").ap(),
        "l0w2": nc.dram_tensor("stg_l0w2", [D, 4096], BF16, kind="Internal").ap(),
        "wq": nc.dram_tensor("stg_wq", [D, D], BF16, kind="Internal").ap(),
        "wk": nc.dram_tensor("stg_wk", [D, D], BF16, kind="Internal").ap(),
        "wv": nc.dram_tensor("stg_wv", [D, D], BF16, kind="Internal").ap(),
        "wo": nc.dram_tensor("stg_wo", [D, D], BF16, kind="Internal").ap(),
        "l1wq": nc.dram_tensor("stg_l1wq", [D, D], BF16, kind="Internal").ap(),
        "l1wk": nc.dram_tensor("stg_l1wk", [D, D], BF16, kind="Internal").ap(),
        "l1wv": nc.dram_tensor("stg_l1wv", [D, D], BF16, kind="Internal").ap(),
        "l1w1": nc.dram_tensor("stg_l1w1", [4096, D], BF16, kind="Internal").ap(),
        "l1w2": nc.dram_tensor("stg_l1w2", [D, 4096], BF16, kind="Internal").ap(),
    }
    rowbounce = nc.dram_tensor("rowbounce", [8, 1024], F32, kind="Internal").ap()
    rowb128 = nc.dram_tensor("rowb128", [8, 128], F32, kind="Internal").ap()
    xb_d = nc.dram_tensor("xb_d", [T, D], F32, kind="Internal").ap()
    cc_in = nc.dram_tensor("cc_in", [128, 128], F32, kind="Internal").ap()
    cc_out = nc.dram_tensor("cc_out", [128, 128], F32, kind="Internal").ap()
    cc_in2 = nc.dram_tensor("cc_in2", [128, 128], F32, kind="Internal").ap()
    cc_out2 = nc.dram_tensor("cc_out2", [128, 128], F32, kind="Internal").ap()
    xlnT_d = nc.dram_tensor("xlnT_d", [D, T], BF16, kind="Internal").ap()
    s_qT = nc.dram_tensor("s_qT", [D, S], BF16, kind="Internal").ap()
    s_kT = nc.dram_tensor("s_kT", [D, S], BF16, kind="Internal").ap()
    s_v = nc.dram_tensor("s_v", [S, D], BF16, kind="Internal").ap()
    xln_d = nc.dram_tensor("xln_d", [T, D], BF16, kind="Internal").ap()
    ysort_d = nc.dram_tensor("ysort_d", [24 * 512, D], F32, kind="Internal").ap()
    tokmap_d = nc.dram_tensor("tokmap_d", [24 * 512, 4], I32, kind="Internal").ap()
    b1T_d = nc.dram_tensor("b1T_d", [1024, 32], F32, kind="Internal").ap()
    b2w_d = nc.dram_tensor("b2w_d", [8, 1024], BF16, kind="Internal").ap()
    moew1_d = nc.dram_tensor("moew1_d", [8192, 4096], BF16, kind="Internal").ap()
    moew2_d = nc.dram_tensor("moew2_d", [32768, 1024], BF16, kind="Internal").ap()

    identb = nc.inline_tensor(np.eye(128, dtype=ml_dtypes.bfloat16), "c_identb").ap()
    identf = nc.inline_tensor(np.eye(128, dtype=np.float32), "c_identf").ap()
    tu = np.triu(np.ones((128, 128), np.float32), 1).astype(ml_dtypes.bfloat16)
    tstrict = nc.inline_tensor(tu, "c_tstrict").ap()
    ones_1x128_b = nc.inline_tensor(np.ones((1, 128), ml_dtypes.bfloat16), "c_ones1b").ap()
    ones_1x128_f = nc.inline_tensor(np.ones((1, 128), np.float32), "c_ones1f").ap()
    ones_128x1_b = nc.inline_tensor(np.ones((128, 1), ml_dtypes.bfloat16), "c_ones128b").ap()
    ones_128x1_f = nc.inline_tensor(np.ones((128, 1), np.float32), "c_ones128f").ap()
    iota8_rows = nc.inline_tensor(np.tile(np.arange(8, dtype=np.float32), (128, 1)), "c_iota8").ap()
    rev8_rows = nc.inline_tensor(np.tile((8.0 - np.arange(8, dtype=np.float32)), (128, 1)), "c_rev8").ap()
    iota128_col = nc.inline_tensor(np.arange(128, dtype=np.float32).reshape(128, 1), "c_iota128").ap()
    bndi_c = nc.inline_tensor(np.array([[T - 1, 24 * 512 - 1]], np.int32), "c_bndi").ap()
    tri8f = nc.inline_tensor(np.triu(np.ones((8, 8), np.float32), 1), "c_tri8f").ap()
    gs24 = nc.inline_tensor((np.arange(24, dtype=np.float32) * 512).reshape(24, 1), "c_gs24").ap()
    _bidx = np.zeros((128, 40), np.float32)
    _midx = np.zeros((128, 40), np.float32)
    _p = np.arange(128, dtype=np.float32)[:, None]
    for _c in range(8):
        _bidx[:, _c:_c + 1] = _c * 128 + _p
        _midx[:, _c] = 1024.0
    for _f in range(32):
        _bidx[:, 8 + _f:9 + _f] = _f * 128 + _p
        _midx[:, 8 + _f] = 4096.0
    bidxw = nc.inline_tensor(_bidx, "c_bidxw").ap()
    multw = nc.inline_tensor(_midx, "c_multw").ap()

    A = nc.alloc_sbuf_tensor
    BREG = {}

    X = [A(f"Xs{i}", [128, D], F32) for i in range(4)]
    XLN = [A("XLN0", [128, D], BF16)]
    XLT = [A("XLT0", [128, 2048], BF16)]
    H = A("H", [128, 4096], BF16)
    WB1 = A("WB1", [128, 32768], BF16)
    WB2 = A("WB2", [128, 32768], BF16)
    SUMS = A("SUMS", [128, NT], F32)
    SUMSQ = A("SUMSQ", [128, NT], F32)
    MU = A("MU", [128, NT], F32)
    RSTD = A("RSTD", [128, NT], F32)
    CONSTC = A("CONSTC", [128, 8], F32)
    THF = A("THF", [128, 2048], F32)
    RW = A("RW", [128, 8], F32)
    OHA = A("OHA", [128, 512], F32)
    RANKA = A("RANKA", [128, 64], F32)
    COMB = A("COMB", [128, 64], F32)
    COEF = A("COEF", [128, 64], F32)
    SLOTI = A("SLOTI", [128, 64], I32)
    LG = A("LG", [128, 8], F32)
    OHB = A("OHB", [128, 8], BF16)
    TMP8 = A("TMP8", [128, 8], F32)
    RUNOFF = A("RUNOFF", [1, 8], F32)
    RUNOFB = A("RUNOFB", [1, 8], F32)
    MROW = A("MROW", [8, 8], F32)
    SEGS = A("SEGS", [24, 8], F32)
    ROW8 = A("ROW8", [1, 8], F32)
    NV8 = A("NV8", [1, 8], F32)
    SS8 = A("SS8", [1, 8], F32)
    I8 = A("I8", [1, 8], I32)
    PCOL = A("PCOL", [8, 1], F32)
    NVB = A("NVB", [128, 8], F32)
    SEGB = A("SEGB", [128, 8], F32)
    OFSB = A("OFSB", [128, 8], F32)
    PIDC = A("PIDC", [128, 1], F32)
    MKC = A("MKC", [128, 1], F32)
    ZB = A("ZB", [128, 128], F32)
    TMAPV = A("TMAPV", [128, 4], I32)
    IDX4 = A("IDX4", [128, 4], I32)
    EB = [A(f"EB{i}", [128, 1024], BF16) for i in range(2)]
    ETABR = A("ETABR", [1, 24], I32)
    TRI8 = A("TRI8", [8, 8], F32)
    GS24T = A("GS24T", [24, 1], F32)
    WGT = A("WGT", [128, 64], BF16)
    B1M = A("B1M", [128, 256], F32)
    PIDT = A("PIDT", [1, 1], mybir.dt.uint32)
    TM96 = A("TM96", [128, 384], I32)
    IDX4B = A("IDX4B", [128, 4], I32)
    B1CUR = A("B1CUR", [128, 32], F32)
    B2CUR = A("B2CUR", [2, 1024], BF16)
    BIDXW = A("BIDXW", [128, 40], F32)
    MULTW = A("MULTW", [128, 40], F32)
    IDXWF = A("IDXWF", [128, 40], F32)
    IDXWI = A("IDXWI", [128, 40], I32)
    EGB = A("EGB", [128, 24], F32)
    ETABF = A("ETABF", [1, 24], F32)
    IDXB = A("IDXB", [128, 1], I32)
    IDXB2 = A("IDXB2", [2, 1], I32)
    BNDI = A("BNDI", [1, 2], I32)
    B1C_L0 = A("B1C_L0", [128, 32], F32)
    B2R_L0 = A("B2R_L0", [1, D], BF16)
    BQC = A("BQC", [128, 8], F32)
    BQE = A("BQE", [128, 8], F32)
    B1EC = A("B1EC", [128, 32], F32)
    B2R_L1 = A("B2R_L1", [1, D], BF16)
    BKER = A("BKER", [1, D], BF16)
    BVER = A("BVER", [1, D], BF16)
    BKEC = A("BKEC", [128, 8], F32)
    BVEC = A("BVEC", [128, 8], F32)
    XR = A("XR", [128, D], F32)
    KVSB = A("KVSB", [128, 1040], BF16)
    KSUM = A("KSUM", [128, 8], BF16)
    ZT = A("ZT", [128, 32], F32)
    DENS = THF
    BKC = A("BKC", [128, 8], F32)
    BVR = A("BVR", [1, D], BF16)
    BOR = A("BOR", [1, D], BF16)
    PF = [X[0], X[1]]
    PFK = ["X0", "X1"]
    PB0 = XLN[0]
    PB0K = "XLN0"
    PS_SC = A("PS_SC", [128, 4], F32)
    IDB = A("IDB", [128, 128], BF16)
    IDF = A("IDF", [128, 128], F32)
    TSTR = A("TSTR", [128, 128], BF16)
    ONE1B = A("ONE1B", [1, 128], BF16)
    ONE1F = A("ONE1F", [1, 128], F32)
    ONEC_B = A("ONEC_B", [128, 1], BF16)
    ONEC_F = A("ONEC_F", [128, 1], F32)
    IOTA8 = A("IOTA8", [128, 8], F32)
    REV8 = A("REV8", [128, 8], F32)
    IOTA128 = A("IOTA128", [128, 1], F32)
    SQJ = XLN[0]; SQJK = "XLN0"

    psS = nc.alloc_psum_tensor("psS", [128, 2048], F32)
    psT = nc.alloc_psum_tensor("psT", [128, 1024], BF16)
    psU = nc.alloc_psum_tensor("psU", [128, 512], F32)
    psV = nc.alloc_psum_tensor("psV", [128, 512], F32)
    psW = nc.alloc_psum_tensor("psW", [128, 512], F32)

    op = em.op

    def load_consts():
        op("sync", lambda e: e.dma_start(out=IDB[:, :], in_=identb[:, :]), writes=["IDB"], dma=True)
        op("sync", lambda e: e.dma_start(out=TSTR[:, :], in_=tstrict[:, :]), writes=["TSTR"], dma=True)
        op("sync", lambda e: e.dma_start(out=IDF[:, :], in_=identf[:, :]), writes=["IDF"], dma=True)
        op("sync", lambda e: e.dma_start(out=ONE1B[:, :], in_=ones_1x128_b[:, :]), writes=["ONE1B"], dma=True)
        op("sync", lambda e: e.dma_start(out=ONE1F[:, :], in_=ones_1x128_f[:, :]), writes=["ONE1F"], dma=True)
        op("sync", lambda e: e.dma_start(out=ONEC_B[:, :], in_=ones_128x1_b[:, :]), writes=["ONEC_B"], dma=True)
        op("sync", lambda e: e.dma_start(out=ONEC_F[:, :], in_=ones_128x1_f[:, :]), writes=["ONEC_F"], dma=True)
        op("sync", lambda e: e.dma_start(out=IOTA8[:, :], in_=iota8_rows[:, :]), writes=["IOTA8"], dma=True)
        op("sync", lambda e: e.dma_start(out=REV8[:, :], in_=rev8_rows[:, :]), writes=["REV8"], dma=True)
        op("sync", lambda e: e.dma_start(out=IOTA128[:, :], in_=iota128_col[:, :]), writes=["IOTA128"], dma=True)
        op("vector", lambda e: e.memset(CONSTC[:, 0:1], EPS_LN), writes=["CONSTC"])
        op("vector", lambda e: e.memset(CONSTC[:, 1:2], 1.0), reads=["CONSTC"], writes=["CONSTC"])
        op("vector", lambda e: e.memset(CONSTC[:, 2:3], EPS_ATTN), reads=["CONSTC"], writes=["CONSTC"])

    def stats_tile(xt_key, xt_tile, j):
        op("scalar",
           lambda e, t=xt_tile, jj=j: e.activation(SQJ[:, :], t[:, :], AF.Square,
                                                   accum_out=SUMSQ[:, jj:jj + 1]),
           reads=[xt_key], writes=[SQJK, f"sumsq{j}"])
        op("vector",
           lambda e, t=xt_tile, jj=j: e.reduce_sum(SUMS[:, jj:jj + 1], t[:, :], axis=AX.X),
           reads=[xt_key], writes=[f"sums{j}"])

    def ln_finalize():
        rd = [f"sums{j}" for j in range(NT)] + [f"sumsq{j}" for j in range(NT)]
        op("vector", lambda e: e.tensor_scalar(MU[:, :], SUMS[:, :], 1.0 / D, None, ALU.mult),
           reads=rd, writes=["MU"])
        op("scalar", lambda e: e.activation(RSTD[:, :], MU[:, :], AF.Square),
           reads=["MU"], writes=["RSTD"])
        op("vector", lambda e: e.scalar_tensor_tensor(RSTD[:, :], SUMSQ[:, :], 1.0 / D,
                                                      RSTD[:, :], ALU.mult, ALU.subtract),
           reads=rd + ["RSTD"], writes=["RSTD"])
        op("scalar", lambda e: e.activation(RSTD[:, :], RSTD[:, :], AF.Ln, bias=CONSTC[:, 0:1]),
           reads=["RSTD", "CONSTC"], writes=["RSTD"])
        op("scalar", lambda e: e.activation(RSTD[:, :], RSTD[:, :], AF.Exp, scale=-0.5),
           reads=["RSTD"], writes=["RSTD"])

    def ln_apply(xk, xt, j, outk, outt):
        op("vector",
           lambda e, t=xt, o=outt, jj=j: e.tensor_scalar(
               o[:, :], t[:, :], MU[:, jj:jj + 1], RSTD[:, jj:jj + 1], ALU.subtract, ALU.mult),
           reads=[xk, "MU", "RSTD"], writes=[outk])

    def transpose_tile(srck, srct, dstk, dstt, dst_cols, tok_off, tok_w=256):
        def tfn(e, s=srct):
            ins = None
            for c in range(8):
                ins = e.transpose(psT[:, c * 128:(c + 1) * 128], s[:, c * 128:(c + 1) * 128], IDB[:, :])
            return ins
        op("tensor", tfn, reads=[srck, "IDB"], writes=["psT"])

        def evfn(e, d=dstt):
            ins = None
            for c in range(8):
                ins = e.tensor_copy(d[:, dst_cols(c) + tok_off: dst_cols(c) + tok_off + 128],
                                    psT[:, c * 128:(c + 1) * 128])
            return ins
        op("vector", evfn, reads=["psT"], writes=[dstk])

    def load_x_slab(sl, xi, src=None, srcname="xbuf"):
        if src is None:
            src = xbuf
        for k in range(2):
            j = sl * 2 + k
            op("scalar",
               lambda e, jj=j, t=X[xi + k], s0=src: e.dma_start(
                   out=t[:, :], in_=s0[jj * 128:(jj + 1) * 128, :]),
               reads=[f"{srcname}{j}"], writes=[f"X{xi + k}"], dma=True, queue="scalar")

    def store_x_slab(sl, xi, dest, destname):
        for k in range(2):
            j = sl * 2 + k
            op("scalar",
               lambda e, jj=j, t=X[xi + k]: e.dma_start(
                   out=dest[jj * 128:(jj + 1) * 128, :], in_=t[:, :]),
               reads=[f"X{xi + k}"], writes=[f"{destname}{j}"], dma=True, queue="scalar")

    def prep_cast(name_src, ap_src, stg_key, rows):
        dst = stg[stg_key]
        op("gpsimd", lambda e: e.dma_start(out=dst[:, :], in_=ap_src[:, :]),
           writes=[f"stg_{stg_key}"], dma=True)

    def prep_ternary(src_ap, stg_key, shape):
        nr, ncol = shape[0] // 128, shape[1] // 1024
        tiles = [(r, c) for r in range(nr) for c in range(ncol)]
        sck = "PS_SC"
        op("vector", lambda e: e.memset(PS_SC[:, 0:1], 0.0), writes=[sck])
        for i, (r, c) in enumerate(tiles):
            pf = PF[i % 2]
            pfk = PFK[i % 2]
            op("sync", lambda e, rr=r, cc=c, t=pf: e.dma_start(
                out=t[:, :], in_=src_ap[rr * 128:(rr + 1) * 128, cc * 1024:(cc + 1) * 1024]),
               writes=[pfk], dma=True)
            op("vector", lambda e, t=pf: e.reduce_sum(PS_SC[:, 1:2], t[:, :], axis=AX.X,
                                                      apply_absolute_value=True),
               reads=[pfk], writes=["PS_SCt"])
            op("vector", lambda e: e.tensor_add(PS_SC[:, 0:1], PS_SC[:, 0:1], PS_SC[:, 1:2]),
               reads=["PS_SC", "PS_SCt"], writes=["PS_SC"])
        op("tensor", lambda e: e.matmul(psU[0:1, 0:1], PS_SC[:, 0:1], ONEC_F[:, :],
                                        start=True, stop=True),
           reads=[sck, "ONEC_F"], writes=["psU"])
        n_el = shape[0] * shape[1]
        op("vector", lambda e: e.tensor_scalar(PS_SC[0:1, 2:3], psU[0:1, 0:1], 1.0 / n_el, 1e-8,
                                               ALU.mult, ALU.max),
           reads=["psU"], writes=["PS_SCthr0"])
        op("vector", lambda e: e.tensor_scalar(PS_SC[0:1, 2:3], PS_SC[0:1, 2:3], 0.5, None, ALU.mult),
           reads=["PS_SCthr0"], writes=["PS_SCthr0"])
        op("tensor", lambda e: e.matmul(psU[:, 1:2], ONE1F[:, :], PS_SC[0:1, 2:3],
                                        start=True, stop=True),
           reads=["PS_SCthr0", "ONE1F"], writes=["psU"])
        op("vector", lambda e: e.tensor_copy(PS_SC[:, 3:4], psU[:, 1:2]),
           reads=["psU"], writes=["PS_SCthr"])
        dst = stg[stg_key]
        for (r, c) in tiles:
            op("sync", lambda e, rr=r, cc=c: e.dma_start(
                out=PF[0][:, :], in_=src_ap[rr * 128:(rr + 1) * 128, cc * 1024:(cc + 1) * 1024]),
               writes=[PFK[0]], dma=True)
            op("scalar", lambda e: e.sign(PF[1][:, :], PF[0][:, :]),
               reads=[PFK[0]], writes=[PFK[1]])
            op("scalar", lambda e: e.activation(PF[0][:, :], PF[0][:, :], AF.Abs),
               reads=[PFK[0], PFK[1]], writes=[PFK[0]])
            op("vector", lambda e: e.tensor_single_scalar(PF[0][:, :], PF[0][:, :],
                                                          PS_SC[:, 3:4], ALU.is_gt),
               reads=[PFK[0], "PS_SCthr"], writes=[PFK[0]])
            op("vector", lambda e: e.tensor_mul(PB0[:, :], PF[0][:, :], PF[1][:, :]),
               reads=[PFK[0], PFK[1]], writes=[PB0K])
            op("gpsimd", lambda e, rr=r, cc=c: e.dma_start(
                out=dst[rr * 128:(rr + 1) * 128, cc * 1024:(cc + 1) * 1024], in_=PB0[:, :]),
               reads=[PB0K], writes=[f"stg_{stg_key}"], dma=True)

    def prep_bias_col(src_ap, n, dstk, dstt):
        op("sync", lambda e: e.dma_start(out=PF[0][0:n, 0:128],
                                         in_=src_ap.rearrange("(c p) -> c p", p=128)),
           writes=[PFK[0]], dma=True)
        op("tensor", lambda e: e.transpose(psW[:, 0:n], PF[0][0:n, 0:128], IDF[0:n, 0:n]),
           reads=[PFK[0], "IDF"], writes=["psW"])
        op("vector", lambda e: e.tensor_copy(dstt[:, 0:n], psW[:, 0:n]),
           reads=["psW"], writes=[dstk])

    def prep_bias_row(src_ap, n, dstk, dstt):
        op("sync", lambda e: e.dma_start(out=PF[1][0:1, 0:n], in_=src_ap.rearrange("(o d) -> o d", o=1)),
           writes=[PFK[1]], dma=True)
        op("vector", lambda e: e.tensor_copy(dstt[0:1, 0:n], PF[1][0:1, 0:n]),
           reads=[PFK[1]], writes=[dstk])

    def prep_stage_cast(src_name, stg_key):
        dst = stg[stg_key]
        op("gpsimd", lambda e: e.dma_start(out=dst[:, :], in_=W[src_name][:, :]),
           writes=[f"stg_{stg_key}"], dma=True)

    def prep_attn_l0():
        for nm, key in (("L0_attn_Wq", "wq"), ("L0_attn_Wk", "wk"),
                        ("L0_attn_Wv", "wv"), ("L0_attn_Wo", "wo")):
            prep_stage_cast(nm, key)
        prep_bias_col(W["L0_attn_bq"], 8, "BQC", BQC)
        prep_bias_col(W["L0_attn_bk"], 8, "BKC", BKC)
        prep_bias_row(W["L0_attn_bv"], D, "BVR", BVR)
        prep_bias_row(W["L0_attn_bo"], D, "BOR", BOR)

    def prep_gscale(src_name, stg_key, g_half, n_rows):
        dst = stg[stg_key]
        for i in range(n_rows // 128):
            op("sync", lambda e, ii=i: e.dma_start(
                out=PF[0][:, :], in_=W[src_name][ii * 128:(ii + 1) * 128, :]),
               writes=[PFK[0]], dma=True)
            op("vector", lambda e, gh=g_half: e.tensor_mul(
                PF[0][:, :], PF[0][:, :], THF[:, gh * 1024:(gh + 1) * 1024]),
               reads=[PFK[0], f"THF{g_half}"], writes=[PFK[0]])
            op("vector", lambda e: e.tensor_copy(PB0[:, :], PF[0][:, :]),
               reads=[PFK[0]], writes=[PB0K])
            op("gpsimd", lambda e, ii=i: e.dma_start(
                out=dst[ii * 128:(ii + 1) * 128, :], in_=PB0[:, :]),
               reads=[PB0K], writes=[f"stg_{stg_key}"], dma=True)

    def prep_bcast_row(src_ap, g_half):
        op("sync", lambda e: e.dma_start(out=PF[1][0:1, 0:1024],
                                         in_=src_ap.rearrange("(o d) -> o d", o=1)),
           writes=[PFK[1]], dma=True)
        for half in range(2):
            op("tensor", lambda e, hh=half: e.matmul(
                psW[:, :], ONE1F[:, :], PF[1][0:1, hh * 512:(hh + 1) * 512],
                start=True, stop=True),
               reads=[PFK[1], "ONE1F"], writes=["psW"])
            op("vector", lambda e, hh=half, gh=g_half: e.tensor_copy(
                THF[:, gh * 1024 + hh * 512: gh * 1024 + (hh + 1) * 512], psW[:, :]),
               reads=["psW"], writes=[f"THF{g_half}"])

    def prep_bias_matvec(src_name, be_half, n_rows, dstt, dstk, addt=None, addk=None):
        nt = n_rows // 128
        for i in range(nt):
            op("sync", lambda e, ii=i: e.dma_start(
                out=PF[0][:, :], in_=W[src_name][ii * 128:(ii + 1) * 128, :]),
               writes=[PFK[0]], dma=True)
            op("vector", lambda e, gh=be_half: e.tensor_mul(
                PF[0][:, :], PF[0][:, :], THF[:, gh * 1024:(gh + 1) * 1024]),
               reads=[PFK[0], f"THF{be_half}"], writes=[PFK[0]])
            op("vector", lambda e, ii=i: e.reduce_sum(dstt[:, ii:ii + 1], PF[0][:, :], axis=AX.X),
               reads=[PFK[0]], writes=[dstk])
        if addt is not None:
            op("vector", lambda e, n0=nt: e.tensor_add(dstt[:, 0:n0], dstt[:, 0:n0], addt[:, 0:n0]),
               reads=[dstk, addk], writes=[dstk])

    def prep_row_from_col(colt, colk, n, dstt, dstk):
        op("tensor", lambda e: e.transpose(psW[0:n, 0:128], colt[:, 0:n], IDF[:, :]),
           reads=[colk, "IDF"], writes=["psW"])
        op("vector", lambda e: e.tensor_copy(PF[0][0:n, 0:128], psW[0:n, 0:128]),
           reads=["psW"], writes=[PFK[0]])
        op("sync", lambda e: e.dma_start(out=rowb128[0:n, :], in_=PF[0][0:n, 0:128]),
           reads=[PFK[0]], writes=["rowb128"], dma=True)
        op("sync", lambda e: e.dma_start(
            out=PF[1][0:1, 0:n * 128], in_=rowb128.rearrange("(o a) b -> o (a b)", o=1)),
           reads=["rowb128"], writes=[PFK[1]], dma=True)
        op("vector", lambda e: e.tensor_copy(dstt[0:1, 0:n * 128], PF[1][0:1, 0:n * 128]),
           reads=[PFK[1]], writes=[dstk])

    def prep_l1():
        prep_bcast_row(W["L1_attn_g1"], 0)
        prep_bcast_row(W["L1_attn_g2"], 1)
        prep_gscale("L1_attn_wq", "l1wq", 0, D)
        prep_gscale("L1_attn_wk", "l1wk", 0, D)
        prep_gscale("L1_attn_wv", "l1wv", 0, D)
        prep_gscale("L1_attn_w1", "l1w1", 1, 4096)
        prep_stage_cast("L1_attn_w2", "l1w2")
        prep_bcast_row(W["L1_attn_be1"], 0)
        prep_bcast_row(W["L1_attn_be2"], 1)
        prep_bias_matvec("L1_attn_wq", 0, D, BQE, "BQE")
        prep_bias_matvec("L1_attn_wk", 0, D, BKEC, "BKEC")
        prep_row_from_col(BKEC, "BKEC", 8, BKER, "BKER")
        prep_bias_matvec("L1_attn_wv", 0, D, BVEC, "BVEC")
        prep_row_from_col(BVEC, "BVEC", 8, BVER, "BVER")
        prep_bias_col(W["L1_attn_b1"], 32, "B1M", B1M)
        prep_bias_matvec("L1_attn_w1", 1, 4096, B1EC, "B1EC", B1M, "B1M")
        prep_bias_row(W["L1_attn_b2"], D, "B2R_L1", B2R_L1)

    def prep_moe():
        op("sync", lambda e: e.dma_start(out=TRI8[:, :], in_=tri8f[:, :]), writes=["TRI8"], dma=True)
        op("sync", lambda e: e.dma_start(out=GS24T[:, :], in_=gs24[:, :]), writes=["GS24T"], dma=True)
        op("sync", lambda e: e.dma_start(out=BIDXW[:, :], in_=bidxw[:, :]), writes=["BIDXW"], dma=True)
        op("sync", lambda e: e.dma_start(out=MULTW[:, :], in_=multw[:, :]), writes=["MULTW"], dma=True)
        for ex in range(8):
            op("gpsimd", lambda e, x0=ex: e.dma_start(
                out=moew1_d[x0 * 1024:(x0 + 1) * 1024, :],
                in_=W["L1_ffn_W1"][x0 * 1024:(x0 + 1) * 1024, :]),
               writes=["moew1_d"], dma=True)
            op("gpsimd", lambda e, x0=ex: e.dma_start(
                out=moew2_d[x0 * 4096:(x0 + 1) * 4096, :],
                in_=W["L1_ffn_W2"][x0 * 4096:(x0 + 1) * 4096, :]),
               writes=["moew2_d"], dma=True)
        op("gpsimd", lambda e: e.dma_start(out=b2w_d[:, :], in_=W["L1_ffn_b2"][:, :]),
           writes=["b2w_d"], dma=True)
        for c in range(8):
            op("gpsimd", lambda e, cc=c: e.dma_start(
                out=WGT[:, cc * 8:(cc + 1) * 8], in_=W["L1_ffn_wg"][cc * 128:(cc + 1) * 128, :]),
               reads=["WGT"], writes=["WGT"], dma=True)
        for ex in range(8):
            op("sync", lambda e, x0=ex: e.dma_start(
                out=PF[0][0:32, 0:128],
                in_=W["L1_ffn_b1"][x0:x0 + 1, :].rearrange("o (c p) -> (o c) p", p=128)),
               writes=[PFK[0]], dma=True)
            op("tensor", lambda e: e.transpose(psW[:, 0:32], PF[0][0:32, 0:128], IDF[0:32, 0:32]),
               reads=[PFK[0], "IDF"], writes=["psW"])
            op("vector", lambda e, x0=ex: e.tensor_copy(B1M[:, x0 * 32:(x0 + 1) * 32], psW[:, 0:32]),
               reads=["psW"], writes=["B1M"])
        assert nc.partition_id_tensor is not None
        pid_ap = nc.partition_id_tensor.ap()
        op("sync", lambda e: e.dma_start(out=PIDT[:, :], in_=pid_ap[0:1, 0:1]),
           writes=["PIDT"], dma=True)
        op("vector", lambda e: e.tensor_copy(TMP8[0:1, 0:1], PIDT[0:1, 0:1]),
           reads=["PIDT"], writes=["TMP8"])
        op("tensor", lambda e: e.matmul(psW[:, 0:1], ONE1F[:, :], TMP8[0:1, 0:1], start=True, stop=True),
           reads=["TMP8", "ONE1F"], writes=["psW"])
        op("vector", lambda e: e.tensor_copy(PIDC[:, :], psW[:, 0:1]), reads=["psW"], writes=["PIDC"])
        op("vector", lambda e: e.tensor_single_scalar(MKC[:, :], IOTA128[:, :], PIDC[:, 0:1], ALU.is_lt),
           reads=["IOTA128", "PIDC"], writes=["MKC"])
        for ex in range(8):
            op("sync", lambda e, x0=ex: e.dma_start(
                out=b1T_d[x0 * 128:(x0 + 1) * 128, :], in_=B1M[:, x0 * 32:(x0 + 1) * 32]),
               reads=["B1M"], writes=["b1T_d"], dma=True)
        op("vector", lambda e: e.memset(ZB[:, :], 0.0), writes=["ZB"])
        op("vector", lambda e: e.memset(TMAPV[:, :], 0), writes=["TMAPV"])
        op("vector", lambda e: e.memset(EB[0][:, :], 0.0), writes=["EB0"])
        op("vector", lambda e: e.memset(EB[1][:, :], 0.0), writes=["EB1"])


    def prep_l0():
        if _SUB >= 1 and _SUB != -2:
            prep_ternary(W["L0_ffn_W1"], "l0w1", (4096, 1024))
        if _SUB == -2:
            prep_ternary(W["L0_ffn_W2"], "l0w2", (1024, 4096))
        if _SUB >= 2:
            prep_ternary(W["L0_ffn_W2"], "l0w2", (1024, 4096))
        if _SUB >= 3:
            prep_bias_col(W["L0_ffn_b1"], 32, "B1C_L0", B1C_L0)
        if _SUB >= 4:
            prep_bias_row(W["L0_ffn_b2"], D, "B2R_L0", B2R_L0)

    def phase_load_x():
        for sl in range(NT // 2):
            xi = (sl % 2) * 2
            for k in range(2):
                j = sl * 2 + k
                op("scalar", lambda e, jj=j, t=X[xi + k]: e.dma_start(
                    out=t[:, :], in_=x_in[jj * 128:(jj + 1) * 128, :]),
                   writes=[f"X{xi + k}"], dma=True, queue="scalar")
                stats_tile(f"X{xi + k}", X[xi + k], j)
                op("scalar", lambda e, jj=j, t=X[xi + k]: e.dma_start(
                    out=xbuf[jj * 128:(jj + 1) * 128, :], in_=t[:, :]),
                   reads=[f"X{xi + k}"], writes=[f"xbuf{j}"], dma=True, queue="scalar")

    def phase_l0ffn(dest, destname):
        phase_ffn(dest, destname, "l0w1", "l0w2", B1C_L0, "B1C_L0", B2R_L0, "B2R_L0", 0.5)

    def phase_l1ffn(dest, destname):
        phase_ffn(dest, destname, "l1w1", "l1w2", B1EC, "B1EC", B2R_L1, "B2R_L1", 1.0,
                  src=xb_d, srcname="xb", extra=xbuf, extraname="xbuf")

    def phase_ffn(dest, destname, stg1, stg2, b1t, b1k, b2t, b2k, scale,
                  src=None, srcname="xbuf", extra=None, extraname=None):
        for c in range(8):
            op("sync", lambda e, cc=c: e.dma_start(
                out=WB1[:, cc * 4096:(cc + 1) * 4096],
                in_=stg[stg1][0:4096, cc * 128:(cc + 1) * 128], transpose=True),
               reads=[f"stg_{stg1}"], writes=[f"WB1_{c}"], dma=True)
        for f in range(32):
            op("sync", lambda e, ff=f: e.dma_start(
                out=WB2[:, ff * 1024:(ff + 1) * 1024],
                in_=stg[stg2][0:1024, ff * 128:(ff + 1) * 128], transpose=True),
               reads=[f"stg_{stg2}"], writes=[f"WB2_{f // 4}"], dma=True)
        ln_finalize()
        wb1keys = [f"WB1_{c}" for c in range(8)]
        wb2keys = [f"WB2_{r}" for r in range(8)]
        for sl in range(NT // 2):
            xi = (sl % 2) * 2
            li = sl % 2
            load_x_slab(sl, xi, src, srcname)
            for k in range(2):
                j = sl * 2 + k
                ln_apply(f"X{xi + k}", X[xi + k], j, "XLN0", XLN[0])
                transpose_tile("XLN0", XLN[0], "XLT0", XLT[0],
                               (lambda c: c * 256), k * 128)
            outps = {(0, 0): psS[:, 1024:1536], (0, 1): psS[:, 1536:2048],
                     (1, 0): psU[:, :], (1, 1): psV[:, :]}
            outpk = {(0, 0): "psS_b2", (0, 1): "psS_b3", (1, 0): "psU", (1, 1): "psV"}
            for half in range(2):
                for fi in range(16):
                    f = half * 16 + fi
                    hp3 = (half * 16 + fi) % 3
                    ps = (psS[:, 0:256], psS[:, 512:768], psW[:, 0:256])[hp3]
                    pk = ("psS_b0", "psS_b1", "psW")[hp3]
                    def hmm(e, ff=f, p=ps):
                        ins = None
                        for c in range(8):
                            ins = e.matmul(p, WB1[:, c * 4096 + ff * 128: c * 4096 + ff * 128 + 128],
                                           XLT[0][:, c * 256:(c + 1) * 256],
                                           start=(c == 0), stop=(c == 7))
                        return ins
                    op("tensor", hmm, reads=["XLT0"] + wb1keys, writes=[pk])
                    op("scalar", lambda e, ff=f, fj=fi, p=ps: e.activation(
                        H[:, fj * 256:(fj + 1) * 256], p, AF.Gelu, bias=B1C_L0[:, ff:ff + 1]),
                       reads=[pk, "B1C_L0"], writes=[f"H{fi}"])
                hkeys = [f"H{fi}" for fi in range(16)]
                for k in range(2):
                    for dh in range(2):
                        ps = outps[(k, dh)]
                        pk = outpk[(k, dh)]
                        def omm(e, kk=k, dd=dh, p=ps, hh=half):
                            ins = None
                            if hh == 0:
                                ins = e.matmul(p, ONE1B[:, :], B2R_L0[0:1, dd * 512:(dd + 1) * 512],
                                               start=True, stop=False)
                            for fi in range(16):
                                f = hh * 16 + fi
                                ins = e.matmul(p, H[:, fi * 256 + kk * 128: fi * 256 + kk * 128 + 128],
                                               WB2[:, f * 1024 + dd * 512: f * 1024 + dd * 512 + 512],
                                               start=False, stop=(hh == 1 and fi == 15))
                            return ins
                        rds = hkeys + wb2keys + ["ONE1B", "B2R_L0"]
                        if half == 1:
                            rds = rds + [pk]
                        op("tensor", omm, reads=rds, writes=[pk])
            for k in range(2):
                j = sl * 2 + k
                for dh in range(2):
                    ps = outps[(k, dh)]
                    pk = outpk[(k, dh)]
                    op("vector", lambda e, dd=dh, p=ps, xt=X[xi + k], sc=scale: e.scalar_tensor_tensor(
                        xt[:, dd * 512:(dd + 1) * 512], p, sc,
                        xt[:, dd * 512:(dd + 1) * 512], ALU.mult, ALU.add),
                       reads=[pk, f"X{xi + k}"], writes=[f"X{xi + k}"])
                if extra is not None:
                    op("scalar", lambda e, jj=j, x0=extra: e.dma_start(
                        out=XR[:, :], in_=x0[jj * 128:(jj + 1) * 128, :]),
                       reads=[f"{extraname}{j}", "XR"], writes=["XR"], dma=True, queue="scalar")
                    op("vector", lambda e, xt=X[xi + k]: e.tensor_add(xt[:, :], xt[:, :], XR[:, :]),
                       reads=[f"X{xi + k}", "XR"], writes=[f"X{xi + k}"])
                stats_tile(f"X{xi + k}", X[xi + k], j)
            store_x_slab(sl, xi, dest, destname)

    def wb1r(c0, c1):
        return [f"WB1_{r}" for r in range(c0 // 4096, (c1 + 4095) // 4096)]

    def wb2r(c0, c1):
        return [f"WB2_{r}" for r in range(c0 // 4096, (c1 + 4095) // 4096)]

    PSS_ALL = ["psS_b0", "psS_b1", "psS_b2", "psS_b3"]

    def stream_w_chunk(stg_key, col0, c):
        op("sync", lambda e, cc=c, co=col0 + c * 1024, ww=stg_key: e.dma_start(
            out=WB1[:, co:co + 1024],
            in_=stg[ww][0:1024, cc * 128:(cc + 1) * 128], transpose=True),
           reads=[f"stg_{stg_key}"], writes=wb1r(col0 + c * 1024, col0 + (c + 1) * 1024), dma=True)

    def phase_l0attn(dest, destname):
        ln_finalize()
        for b in range(NB):
            jb = b * 16
            if b == 0:
                for wi, wn in enumerate(["wq", "wk", "wv", "wo"]):
                    for c in range(8):
                        stream_w_chunk(wn, wi * 8192, c)
            else:
                for c in range(8):
                    stream_w_chunk("wq", 0, c)
                for c in range(4):
                    stream_w_chunk("wk", 8192, c)
            for sl in range(8):
                xi = (sl % 2) * 2
                for k in range(2):
                    j = jb + sl * 2 + k
                    op("scalar", lambda e, jj=j, t=X[xi + k]: e.dma_start(
                        out=t[:, :], in_=xbuf[jj * 128:(jj + 1) * 128, :]),
                       reads=[f"xbuf{j}"], writes=[f"X{xi + k}"], dma=True, queue="scalar")
                    ln_apply(f"X{xi + k}", X[xi + k], j, "XLN0", XLN[0])
                    transpose_tile("XLN0", XLN[0], "XLT0", XLT[0],
                                   (lambda c: c * 256), k * 128)
                for wi, (dst_d, bias_c, bias_k) in enumerate(
                        ((s_qT, BQC, "BQC"), (s_kT, BKC, "BKC"))):
                    for ec in range(8):
                        ps = psU if ec % 2 == 0 else psV
                        pk = "psU" if ec % 2 == 0 else "psV"
                        def qmm(e, w0=wi * 8192, ee=ec, p=ps):
                            ins = None
                            for c in range(8):
                                ins = e.matmul(p[:, 0:256],
                                               WB1[:, w0 + c * 1024 + ee * 128: w0 + c * 1024 + ee * 128 + 128],
                                               XLT[0][:, c * 256:(c + 1) * 256],
                                               start=(c == 0), stop=(c == 7))
                            return ins
                        op("tensor", qmm, reads=["XLT0"] + wb1r(wi * 8192, (wi + 1) * 8192), writes=[pk])
                        op("scalar", lambda e, ee=ec, p=ps, bc=bias_c: e.activation(
                            H[:, ee * 256:(ee + 1) * 256], p[:, 0:256], AF.Identity, bias=bc[:, ee:ee + 1]),
                           reads=[pk, bias_k], writes=[f"H{ec}"])
                    def qst(e, d=dst_d, ss=sl):
                        return [e.dma_start(out=d[ec * 128:(ec + 1) * 128, ss * 256:(ss + 1) * 256],
                                            in_=H[:, ec * 256:(ec + 1) * 256])
                                for ec in range(8)]
                    op("sync", qst, reads=[f"H{ec}" for ec in range(8)],
                       writes=[f"sqkT{wi}_{sl}"], dma=True, n_dma=8)
                for k in range(2):
                    for dh in range(2):
                        ps = psS[:, 1024 + dh * 512: 1536 + dh * 512]
                        pk = f"psS_b{2 + dh}"
                        def vmm(e, kk=k, dd=dh, p=ps):
                            ins = e.matmul(p, ONE1B[:, :], BVR[0:1, dd * 512:(dd + 1) * 512],
                                           start=True, stop=False)
                            for c in range(8):
                                ins = e.matmul(p, XLT[0][:, c * 256 + kk * 128: c * 256 + kk * 128 + 128],
                                               WB1[:, 16384 + c * 1024 + dd * 512: 16384 + c * 1024 + dd * 512 + 512],
                                               start=False, stop=(c == 7))
                            return ins
                        op("tensor", vmm, reads=["XLT0", "ONE1B", "BVR"] + wb1r(16384, 24576), writes=[pk])
                        op("vector", lambda e, dd=dh, p=ps: e.tensor_copy(
                            XLN[0][:, dd * 512:(dd + 1) * 512], p),
                           reads=[pk], writes=["XLN0"])
                    op("sync", lambda e, ss=sl, kk=k: e.dma_start(
                        out=s_v[ss * 256 + kk * 128: ss * 256 + kk * 128 + 128, :], in_=XLN[0][:, :]),
                       reads=["XLN0"], writes=[f"sv{sl}_{k}"], dma=True)
            for c in range(8):
                op("sync", lambda e, cc=c: e.dma_start(
                    out=WB2[:, cc * 2048:(cc + 1) * 2048], in_=s_kT[cc * 128:(cc + 1) * 128, :]),
                   reads=[f"sqkT1_{ss}" for ss in range(8)],
                   writes=wb2r(c * 2048, (c + 1) * 2048), dma=True)
            for t_ in range(16):
                op("sync", lambda e, tt=t_: e.dma_start(
                    out=WB2[:, 16384 + tt * 1024: 16384 + (tt + 1) * 1024],
                    in_=s_v[tt * 128:(tt + 1) * 128, :]),
                   reads=[f"sv{t_ // 2}_{t_ % 2}"],
                   writes=wb2r(16384 + t_ * 1024, 16384 + (t_ + 1) * 1024), dma=True)
            kT_keys = wb2r(0, 16384)
            v_keys = wb2r(16384, 32768)
            for g in range(4):
                for c in range(8):
                    op("sync", lambda e, cc=c, gg=g: e.dma_start(
                        out=WB1[:, cc * 512:(cc + 1) * 512],
                        in_=s_qT[cc * 128:(cc + 1) * 128, gg * 512:(gg + 1) * 512]),
                       reads=[f"sqkT0_{ss}" for ss in range(8)], writes=wb1r(0, 4096), dma=True)
                for ci in range(4):
                    def smm(e, cc=ci):
                        ins = None
                        for nb4 in range(4):
                            for c in range(8):
                                ins = e.matmul(psS[:, nb4 * 512:(nb4 + 1) * 512],
                                               WB1[:, c * 512 + cc * 128: c * 512 + cc * 128 + 128],
                                               WB2[:, c * 2048 + nb4 * 512: c * 2048 + nb4 * 512 + 512],
                                               start=(c == 0), stop=(c == 7))
                        return ins
                    op("tensor", smm, reads=wb1r(0, 4096) + kT_keys, writes=PSS_ALL)
                    for h2 in range(2):
                        xt = X[2 + h2]
                        xk = f"X{2 + h2}"
                        sl_ = slice(h2 * 1024, (h2 + 1) * 1024)
                        op("scalar", lambda e, t=xt, s0=sl_: e.activation(
                            t[:, :], psS[:, s0], AF.Tanh, scale=0.5),
                           reads=PSS_ALL, writes=[xk])
                        op("scalar", lambda e, s0=sl_, hh=h2: e.activation(
                            THF[:, hh * 1024:(hh + 1) * 1024], psS[:, s0], AF.Tanh),
                           reads=PSS_ALL, writes=[f"THF{h2}"])
                        op("vector", lambda e, t=xt: e.tensor_scalar(
                            t[:, :], t[:, :], 0.1, 0.6, ALU.mult, ALU.add),
                           reads=[xk], writes=[xk])
                        op("vector", lambda e, t=xt, hh=h2: e.scalar_tensor_tensor(
                            t[:, :], THF[:, hh * 1024:(hh + 1) * 1024], 0.15, t[:, :],
                            ALU.mult, ALU.add),
                           reads=[xk, f"THF{h2}"], writes=[xk])
                        op("scalar", lambda e, s0=sl_, hh=h2: e.activation(
                            THF[:, hh * 1024:(hh + 1) * 1024], psS[:, s0], AF.Relu, scale=0.1),
                           reads=PSS_ALL + [f"THF{h2}"], writes=[f"THF{h2}"])
                        op("gpsimd", lambda e, t=xt, hh=h2: e.tensor_add(
                            t[:, :], t[:, :], THF[:, hh * 1024:(hh + 1) * 1024]),
                           reads=[xk, f"THF{h2}"], writes=[xk])
                        op("vector", lambda e, t=xt, hh=h2: e.reduce_max(
                            RW[:, hh:hh + 1], t[:, :], axis=AX.X, negate=True),
                           reads=[xk], writes=[f"RW{h2}"])
                    op("vector", lambda e: e.tensor_tensor(RW[:, 2:3], RW[:, 0:1], RW[:, 1:2], ALU.min),
                       reads=["RW0", "RW1"], writes=["RW2"])
                    for h2 in range(2):
                        xt = X[2 + h2]
                        xk = f"X{2 + h2}"
                        op("scalar", lambda e, t=xt, hh=h2: e.activation(
                            XLT[0][:, hh * 1024:(hh + 1) * 1024], t[:, :], AF.Exp,
                            bias=RW[:, 2:3], accum_out=RW[:, 4 + hh:5 + hh]),
                           reads=[xk, "RW2"], writes=["XLT0", f"RW{4 + h2}"])
                    op("vector", lambda e: e.tensor_add(RW[:, 6:7], RW[:, 4:5], RW[:, 5:6]),
                       reads=["RW4", "RW5"], writes=["RW6"])
                    op("vector", lambda e: e.reciprocal(RW[:, 7:8], RW[:, 6:7]),
                       reads=["RW6"], writes=["RW7"])
                    for h2 in range(2):
                        op("vector", lambda e, hh=h2: e.tensor_scalar_mul(
                            XLT[0][:, hh * 1024:(hh + 1) * 1024],
                            XLT[0][:, hh * 1024:(hh + 1) * 1024], RW[:, 7:8]),
                           reads=["XLT0", "RW7"], writes=["XLT0"])
                    for tb in range(2):
                        def tfn(e, t0=tb * 8):
                            ins = None
                            for u in range(8):
                                ins = e.transpose(psT[:, u * 128:(u + 1) * 128],
                                                  XLT[0][:, (t0 + u) * 128:(t0 + u + 1) * 128],
                                                  IDB[:, :])
                            return ins
                        op("tensor", tfn, reads=["XLT0", "IDB"], writes=["psT"])
                        def pev(e, t0=tb * 8, cc=ci):
                            ins = None
                            for u in range(8):
                                ins = e.tensor_copy(
                                    WB1[:, 4096 + (t0 + u) * 512 + cc * 128: 4096 + (t0 + u) * 512 + cc * 128 + 128],
                                    psT[:, u * 128:(u + 1) * 128])
                            return ins
                        op("vector", pev, reads=["psT"], writes=wb1r(4096, 12288))
                for dc in range(8):
                    ps = psU if dc % 2 == 0 else psV
                    pk = "psU" if dc % 2 == 0 else "psV"
                    def avmm(e, d0=dc, p=ps):
                        ins = None
                        for t_ in range(16):
                            ins = e.matmul(p[:, :],
                                           WB2[:, 16384 + t_ * 1024 + d0 * 128: 16384 + t_ * 1024 + d0 * 128 + 128],
                                           WB1[:, 4096 + t_ * 512: 4096 + (t_ + 1) * 512],
                                           start=(t_ == 0), stop=(t_ == 15))
                        return ins
                    op("tensor", avmm, reads=v_keys + wb1r(4096, 12288), writes=[pk])
                    op("vector", lambda e, d0=dc, p=ps: e.tensor_copy(
                        H[:, d0 * 512:(d0 + 1) * 512], p[:, :]),
                       reads=[pk], writes=[f"H{2 * dc}", f"H{2 * dc + 1}"])
                for ci in range(4):
                    jj = jb + g * 4 + ci
                    xi2 = ci % 2
                    op("scalar", lambda e, j0=jj, t=X[xi2]: e.dma_start(
                        out=t[:, :], in_=xbuf[j0 * 128:(j0 + 1) * 128, :]),
                       reads=[f"xbuf{jj}"], writes=[f"X{xi2}"], dma=True, queue="scalar")
                    for dh in range(2):
                        ps = psS[:, 1024 + dh * 512: 1536 + dh * 512]
                        pk = f"psS_b{2 + dh}"
                        def wmm(e, cc=ci, dd=dh, p=ps):
                            ins = e.matmul(p, ONE1B[:, :], BOR[0:1, dd * 512:(dd + 1) * 512],
                                           start=True, stop=False)
                            for c in range(8):
                                ins = e.matmul(p, H[:, c * 512 + cc * 128: c * 512 + cc * 128 + 128],
                                               WB1[:, 24576 + c * 1024 + dd * 512: 24576 + c * 1024 + dd * 512 + 512],
                                               start=False, stop=(c == 7))
                            return ins
                        op("tensor", wmm,
                           reads=[f"H{u}" for u in range(16)] + ["ONE1B", "BOR"] + wb1r(24576, 32768),
                           writes=[pk])
                        op("vector", lambda e, dd=dh, p=ps, t=X[xi2]: e.scalar_tensor_tensor(
                            t[:, dd * 512:(dd + 1) * 512], p, 1.0,
                            t[:, dd * 512:(dd + 1) * 512], ALU.mult, ALU.add),
                           reads=[pk, f"X{xi2}"], writes=[f"X{xi2}"])
                    stats_tile(f"X{xi2}", X[xi2], jj)
                    op("scalar", lambda e, j0=jj, t=X[xi2]: e.dma_start(
                        out=dest[j0 * 128:(j0 + 1) * 128, :], in_=t[:, :]),
                       reads=[f"X{xi2}"], writes=[f"{destname}{jj}"], dma=True, queue="scalar")

    def store_dummy(dest, destname):
        for j in range(NT):
            op("scalar", lambda e, jj=j: e.dma_start(
                out=dest[jj * 128:(jj + 1) * 128, :], in_=xbuf[jj * 128:(jj + 1) * 128, :]),
               reads=[f"xbuf{j}"], writes=[f"{destname}{j}"], dma=True, queue="scalar")

    def phase_l1pre():
        ln_finalize()
        for sl in range(NT // 2):
            xi = (sl % 2) * 2
            load_x_slab(sl, xi)
            for k in range(2):
                j = sl * 2 + k
                op("vector", lambda e, t=X[xi + k], jj=j: e.tensor_scalar(
                    t[:, :], t[:, :], MU[:, jj:jj + 1], RSTD[:, jj:jj + 1],
                    ALU.subtract, ALU.mult),
                   reads=[f"X{xi + k}", "MU", "RSTD"], writes=[f"X{xi + k}"])
                stats_tile(f"X{xi + k}", X[xi + k], j)
            store_x_slab(sl, xi, xb_d, "xb")

    def phase_l1attn():
        ln_finalize()
        for b in range(NB if _L1P >= 2 else 0):
            jb = b * 16
            for wi, wn in enumerate(["l1wq", "l1wk", "l1wv"]):
                for c in range(8):
                    stream_w_chunk(wn, wi * 8192, c)
            for sl in range(8 if _L1A >= 2 else 0):
                xi = (sl % 2) * 2
                for k in range(2):
                    j = jb + sl * 2 + k
                    op("scalar", lambda e, jj=j, t=X[xi + k]: e.dma_start(
                        out=t[:, :], in_=xb_d[jj * 128:(jj + 1) * 128, :]),
                       reads=[f"xb{j}"], writes=[f"X{xi + k}"], dma=True, queue="scalar")
                    ln_apply(f"X{xi + k}", X[xi + k], j, "XLN0", XLN[0])
                    transpose_tile("XLN0", XLN[0], "XLT0", XLT[0],
                                   (lambda c: c * 256), k * 128)
                for ec in range(8 if _L1A >= 3 else 0):
                    ps = psU if ec % 2 == 0 else psV
                    pk = "psU" if ec % 2 == 0 else "psV"
                    def qmm(e, ee=ec, p=ps):
                        ins = None
                        for c in range(8):
                            ins = e.matmul(p[:, 0:256],
                                           WB1[:, c * 1024 + ee * 128: c * 1024 + ee * 128 + 128],
                                           XLT[0][:, c * 256:(c + 1) * 256],
                                           start=(c == 0), stop=(c == 7))
                        return ins
                    op("tensor", qmm, reads=["XLT0"] + wb1r(0, 8192), writes=[pk])
                    op("scalar", lambda e, ee=ec, p=ps: e.activation(
                        THF[:, 0:256], p[:, 0:256], AF.Relu, bias=BQE[:, ee:ee + 1]),
                       reads=[pk, "BQE", "THF0"], writes=["THF0"])
                    op("vector", lambda e, ee=ec, p=ps: e.tensor_scalar(
                        THF[:, 1024:1280], p[:, 0:256], BQE[:, ee:ee + 1], 0.0,
                        ALU.add, ALU.min),
                       reads=[pk, "BQE", "THF1"], writes=["THF1"])
                    op("scalar", lambda e: e.activation(THF[:, 1024:1280], THF[:, 1024:1280], AF.Exp),
                       reads=["THF1"], writes=["THF1"])
                    op("vector", lambda e, ee=ec: e.tensor_add(
                        H[:, ee * 256:(ee + 1) * 256], THF[:, 0:256], THF[:, 1024:1280]),
                       reads=["THF0", "THF1"], writes=[f"H{ec}"])
                if _L1A >= 3:
                    def qst(e, ss=sl):
                        return [e.dma_start(out=s_qT[ec * 128:(ec + 1) * 128, ss * 256:(ss + 1) * 256],
                                            in_=H[:, ec * 256:(ec + 1) * 256])
                                for ec in range(8)]
                    op("sync", qst, reads=[f"H{ec}" for ec in range(8)],
                       writes=[f"sqkT0_{sl}"], dma=True, n_dma=8)
                for wv_, (w0, seedr, seedk, cbase, do_elu) in (enumerate(
                        ((8192, BKER, "BKER", 16384, True), (16384, BVER, "BVER", 0, False)))
                        if _L1A >= 4 else []):
                    for k in range(2):
                        tt = sl * 2 + k
                        for dh in range(2):
                            ps = psS[:, dh * 512:(dh + 1) * 512] if wv_ == 0 else psS[:, 1024 + dh * 512:1536 + dh * 512]
                            pk = f"psS_b{dh}" if wv_ == 0 else f"psS_b{2 + dh}"
                            def kvm(e, kk=k, dd=dh, p=ps, ww=w0, sr=seedr):
                                ins = e.matmul(p, ONE1B[:, :], sr[0:1, dd * 512:(dd + 1) * 512],
                                               start=True, stop=False)
                                for c in range(8):
                                    ins = e.matmul(p, XLT[0][:, c * 256 + kk * 128: c * 256 + kk * 128 + 128],
                                                   WB1[:, ww + c * 1024 + dd * 512: ww + c * 1024 + dd * 512 + 512],
                                                   start=False, stop=(c == 7))
                                return ins
                            op("tensor", kvm, reads=["XLT0", "ONE1B", seedk] + wb1r(w0, w0 + 8192),
                               writes=[pk])
                            dstc = cbase + tt * 1024 + dh * 512
                            if do_elu:
                                op("scalar", lambda e, p=ps: e.activation(
                                    THF[:, 0:512], p, AF.Relu),
                                   reads=[pk, "THF0"], writes=["THF0"])
                                op("vector", lambda e, p=ps: e.tensor_scalar(
                                    THF[:, 1024:1536], p, 0.0, None, ALU.min),
                                   reads=[pk, "THF1"], writes=["THF1"])
                                op("scalar", lambda e: e.activation(
                                    THF[:, 1024:1536], THF[:, 1024:1536], AF.Exp),
                                   reads=["THF1"], writes=["THF1"])
                                op("vector", lambda e, d0=dstc: e.tensor_add(
                                    WB2[:, d0:d0 + 512], THF[:, 0:512], THF[:, 1024:1536]),
                                   reads=["THF0", "THF1"], writes=wb2r(dstc, dstc + 512))
                            else:
                                op("vector", lambda e, d0=dstc, p=ps: e.tensor_copy(
                                    WB2[:, d0:d0 + 512], p),
                                   reads=[pk], writes=wb2r(dstc, dstc + 512))
            kv_keys = wb2r(0, 32768)
            for h in range(8 if _L1P >= 3 else 0):
                for t_ in range(16):
                    op("vector", lambda e, hh=h, tt=t_: e.tensor_copy(
                        XLN[0][:, 0:128], WB2[:, tt * 1024 + hh * 128: tt * 1024 + (hh + 1) * 128]),
                       reads=kv_keys + ["XLN0"], writes=["XLN0"])
                    if t_ == 0 and h == 0:
                        op("vector", lambda e: e.memset(XLN[0][:, 128:129], 1.0),
                           reads=["XLN0"], writes=["XLN0x"])
                    op("tensor", lambda e, hh=h, tt=t_: e.matmul(
                        psW[:, 0:129],
                        WB2[:, 16384 + tt * 1024 + hh * 128: 16384 + tt * 1024 + (hh + 1) * 128],
                        XLN[0][:, 0:129],
                        start=(tt == 0), stop=(tt == 15)),
                       reads=kv_keys + ["XLN0", "XLN0x"] + (["psW"] if t_ > 0 else []),
                       writes=["psW"])
                op("vector", lambda e, hh=h: e.tensor_copy(
                    KVSB[:, hh * 129: hh * 129 + 128], psW[:, 0:128]),
                   reads=["psW"], writes=["KVSB"])
                op("vector", lambda e, hh=h: e.tensor_copy(
                    KSUM[:, hh:hh + 1], psW[:, 128:129]),
                   reads=["psW"], writes=["KSUM"])
            for g in range(4 if _L1P >= 4 else 0):
                for hs in range(2):
                    def qld(e, gg=g, h0=hs * 4):
                        return [e.dma_start(out=XLT[0][:, c * 512:(c + 1) * 512],
                                            in_=s_qT[(h0 + c) * 128:(h0 + c + 1) * 128,
                                                     gg * 512:(gg + 1) * 512])
                                for c in range(4)]
                    op("sync", qld, reads=[f"sqkT0_{ss}" for ss in range(8)] + ["XLT0"],
                       writes=["XLT0"], dma=True, n_dma=4)
                    for c in range(4):
                        h = hs * 4 + c
                        op("tensor", lambda e, cc=c, hh=h: e.matmul(
                            psU[0:1, :], KSUM[:, hh:hh + 1],
                            XLT[0][:, cc * 512:(cc + 1) * 512],
                            start=True, stop=True),
                           reads=["KSUM", "XLT0"], writes=["psU"])
                        op("vector", lambda e, cc=c: e.tensor_copy(
                            THF[0:1, cc * 512:(cc + 1) * 512], psU[0:1, :]),
                           reads=["psU"], writes=["THF0", "THF1"])
                    def dtr(e, h0=hs * 4):
                        ins = None
                        for u in range(4):
                            for cc in range(4):
                                ins = e.transpose(psV[:, u * 8 + h0 + cc: u * 8 + h0 + cc + 1],
                                                  THF[0:1, cc * 512 + u * 128: cc * 512 + (u + 1) * 128],
                                                  IDF[0:1, 0:1])
                        return ins
                    op("tensor", dtr, reads=["THF0", "THF1", "IDF"], writes=["psV"])
                    op("vector", lambda e: e.tensor_scalar(ZT[:, :], psV[:, 0:32], EPS_ATTN, None, ALU.add),
                       reads=["psV"], writes=["ZTraw", "ZT"])
                    op("vector", lambda e: e.reciprocal(ZT[:, :], ZT[:, :]),
                       reads=["ZTraw"], writes=["ZT"])
                    for u in range(4):
                        j = jb + g * 4 + u
                        if hs == 0:
                            op("scalar", lambda e, jj=j, t=X[u]: e.dma_start(
                                out=t[:, :], in_=xb_d[jj * 128:(jj + 1) * 128, :]),
                               reads=[f"xb{j}"], writes=[f"X{u}"], dma=True, queue="scalar")
                        for c in range(4):
                            h = hs * 4 + c
                            bank = c % 2
                            pk = f"psS_b{bank}"
                            op("tensor", lambda e, hh=h, cc=c, uu=u, bb=bank: e.matmul(
                                psS[:, bb * 512 + (cc // 2) * 128: bb * 512 + (cc // 2) * 128 + 128],
                                XLT[0][:, cc * 512 + uu * 128: cc * 512 + uu * 128 + 128],
                                KVSB[:, hh * 129: hh * 129 + 128],
                                start=True, stop=True),
                               reads=["XLT0", "KVSB"], writes=[pk])
                            op("vector", lambda e, hh=h, cc=c, uu=u, bb=bank, t=X[u]: e.scalar_tensor_tensor(
                                t[:, hh * 128:(hh + 1) * 128],
                                psS[:, bb * 512 + (cc // 2) * 128: bb * 512 + (cc // 2) * 128 + 128],
                                ZT[:, uu * 8 + hh: uu * 8 + hh + 1],
                                t[:, hh * 128:(hh + 1) * 128], ALU.mult, ALU.add),
                               reads=[pk, "ZT", f"X{u}"], writes=[f"X{u}"])
                        if hs == 1:
                            stats_tile(f"X{u}", X[u], j)
                            op("scalar", lambda e, jj=j, t=X[u]: e.dma_start(
                                out=xb_d[jj * 128:(jj + 1) * 128, :], in_=t[:, :]),
                               reads=[f"X{u}"], writes=[f"xb{j}"], dma=True, queue="scalar")

    def phase_moe(dest, destname, ci_, co_, callid):
        ln_finalize()
        op("vector", lambda e: e.memset(RUNOFF[:, :], 0.0), writes=["RUNOFF"])
        for sl in range(NT // 2):
            xi = (sl % 2) * 2
            load_x_slab(sl, xi)
            for k in range(2):
                j = sl * 2 + k
                ln_apply(f"X{xi + k}", X[xi + k], j, "XLN0", XLN[0])
                op("sync", lambda e, jj=j: e.dma_start(
                    out=xln_d[jj * 128:(jj + 1) * 128, :], in_=XLN[0][:, :]),
                   reads=["XLN0"], writes=[f"xln{j}"], dma=True)
                transpose_tile("XLN0", XLN[0], "XLT0", XLT[0], (lambda c: c * 256), k * 128)
            for k in range(2):
                j = sl * 2 + k
                def lmm(e, kk=k):
                    ins = None
                    for c in range(8):
                        ins = e.matmul(psW[:, 0:8],
                                       XLT[0][:, c * 256 + kk * 128: c * 256 + kk * 128 + 128],
                                       WGT[:, c * 8:(c + 1) * 8],
                                       start=(c == 0), stop=(c == 7))
                    return ins
                op("tensor", lmm, reads=["XLT0", "WGT"], writes=["psW"])
                op("vector", lambda e: e.tensor_copy(LG[:, :], psW[:, 0:8]),
                   reads=["psW"], writes=["LG"])
                op("vector", lambda e: e.reduce_max(RW[:, 0:1], LG[:, :], axis=AX.X),
                   reads=["LG"], writes=["RW0"])
                op("vector", lambda e: e.reduce_max(RW[:, 1:2], LG[:, :], axis=AX.X, negate=True),
                   reads=["LG"], writes=["RW1"])
                op("scalar", lambda e, jj=j: e.activation(TMP8[:, :], LG[:, :], AF.Exp,
                                                          bias=RW[:, 1:2], accum_out=RW[:, 2:3]),
                   reads=["LG", "RW1"], writes=["TMP8", "RW2"])
                op("vector", lambda e, jj=j: e.reciprocal(COMB[:, jj:jj + 1], RW[:, 2:3]),
                   reads=["RW2"], writes=["COMB"])
                op("vector", lambda e: e.tensor_single_scalar(TMP8[:, :], LG[:, :], RW[:, 0:1], ALU.is_equal),
                   reads=["LG", "RW0"], writes=["TMP8"])
                op("vector", lambda e: e.tensor_mul(TMP8[:, :], TMP8[:, :], REV8[:, :]),
                   reads=["TMP8", "REV8"], writes=["TMP8"])
                op("vector", lambda e: e.reduce_max(RW[:, 3:4], TMP8[:, :], axis=AX.X),
                   reads=["TMP8"], writes=["RW3"])
                op("vector", lambda e, jj=j: e.tensor_single_scalar(
                    OHA[:, jj * 8:(jj + 1) * 8], TMP8[:, :], RW[:, 3:4], ALU.is_equal),
                   reads=["TMP8", "RW3"], writes=["OHA"])
                op("vector", lambda e, jj=j: e.tensor_copy(OHB[:, :], OHA[:, jj * 8:(jj + 1) * 8]),
                   reads=["OHA"], writes=["OHB"])
                op("tensor", lambda e: e.matmul(psV[:, 0:8], TSTR[:, :], OHB[:, :],
                                                start=True, stop=False),
                   reads=["TSTR", "OHB"], writes=["psV"])
                op("tensor", lambda e: e.matmul(psV[:, 0:8], ONE1F[:, :], RUNOFF[0:1, :],
                                                start=False, stop=True),
                   reads=["ONE1F", "RUNOFF"], writes=["psV"])
                op("vector", lambda e, jj=j: e.tensor_mul(TMP8[:, :], OHA[:, jj * 8:(jj + 1) * 8], psV[:, 0:8]),
                   reads=["OHA", "psV"], writes=["TMP8"])
                op("vector", lambda e, jj=j: e.reduce_sum(RANKA[:, jj:jj + 1], TMP8[:, :], axis=AX.X),
                   reads=["TMP8"], writes=["RANKA"])
                op("tensor", lambda e: e.matmul(psU[0:1, 0:8], ONEC_B[:, :], OHB[:, :],
                                                start=True, stop=True),
                   reads=["ONEC_B", "OHB"], writes=["psU"])
                op("vector", lambda e: e.tensor_add(RUNOFF[0:1, :], RUNOFF[0:1, :], psU[0:1, 0:8]),
                   reads=["RUNOFF", "psU"], writes=["RUNOFF"])
        op("gpsimd", lambda e: e.dma_start(out=ci_[:, :], in_=ZB[:, :]),
           reads=["ZB"], writes=["ccin"], dma=True)
        op("gpsimd", lambda e: e.dma_start(
            out=ci_[bass.ts(e.partition_id(), 1), 0:8], in_=RUNOFF[0:1, :]),
           reads=["RUNOFF", "ccin"], writes=["ccin"], dma=True)
        op("gpsimd", lambda e: e.collective_compute(
            "AllReduce", ALU.add, replica_groups=[list(range(NCORE))],
            ins=[ci_[:, :].opt()], outs=[co_[:, :].opt()]),
           reads=["ccin"], writes=["ccout"], cc=True)
        op("gpsimd", lambda e: e.dma_start(out=MROW[:, :], in_=co_[0:8, 0:8]),
           reads=["ccout"], writes=["MROW"], dma=True)
        op("tensor", lambda e: e.matmul(psU[0:1, 0:8], MKC[0:8, 0:1], MROW[:, :],
                                        start=True, stop=True),
           reads=["MKC", "MROW"], writes=["psU"])
        op("vector", lambda e: e.tensor_copy(ROW8[:, :], psU[0:1, 0:8]),
           reads=["psU"], writes=["ROW8"])
        op("vector", lambda e: e.tensor_scalar(NV8[:, :], ROW8[:, :], -1.0, float(CAP),
                                               ALU.mult, ALU.add),
           reads=["ROW8"], writes=["NV8"])
        op("vector", lambda e: e.tensor_scalar(NV8[:, :], NV8[:, :], 0.0, None, ALU.max),
           reads=["NV8"], writes=["NV8"])
        op("vector", lambda e: e.tensor_tensor(NV8[:, :], NV8[:, :], RUNOFF[0:1, :], ALU.min),
           reads=["NV8", "RUNOFF"], writes=["NV8"])
        op("vector", lambda e: e.tensor_scalar(TMP8[0:1, 0:8], NV8[:, :], 1.0 / 512.0,
                                               511.0 / 512.0 - 0.49999, ALU.mult, ALU.add),
           reads=["NV8"], writes=["TMP8"])
        op("vector", lambda e: e.tensor_copy(I8[:, :], TMP8[0:1, 0:8]), reads=["TMP8"], writes=["I8"])
        op("vector", lambda e: e.tensor_copy(TMP8[0:1, 0:8], I8[:, :]), reads=["I8"], writes=["TMP8"])
        op("vector", lambda e: e.tensor_scalar(TMP8[0:1, 0:8], TMP8[0:1, 0:8], 512.0, None, ALU.mult),
           reads=["TMP8"], writes=["TMP8"])
        op("tensor", lambda e: e.transpose(psW[0:8, 0:1], TMP8[0:1, 0:8], IDF[0:1, 0:1]),
           reads=["TMP8", "IDF"], writes=["psW"])
        op("vector", lambda e: e.tensor_copy(PCOL[:, :], psW[0:8, 0:1]), reads=["psW"], writes=["PCOL"])
        op("tensor", lambda e: e.matmul(psU[0:8, 0:1], TRI8[:, :], PCOL[:, :], start=True, stop=True),
           reads=["TRI8", "PCOL"], writes=["psU"])
        op("vector", lambda e: e.tensor_copy(PCOL[:, :], psU[0:8, 0:1]), reads=["psU"], writes=["PCOL"])
        op("tensor", lambda e: e.transpose(psW[0:1, 0:8], PCOL[0:8, 0:1], IDF[0:8, 0:8]),
           reads=["PCOL", "IDF"], writes=["psW"])
        op("vector", lambda e: e.tensor_copy(SS8[:, :], psW[0:1, 0:8]), reads=["psW"], writes=["SS8"])
        for src, srk, dstt, dstk in ((NV8, "NV8", NVB, "NVB"), (SS8, "SS8", SEGB, "SEGB"),
                                     (ROW8, "ROW8", OFSB, "OFSB")):
            op("tensor", lambda e, s0=src: e.matmul(psW[:, 0:8], ONE1F[:, :], s0[0:1, :],
                                                    start=True, stop=True),
               reads=[srk, "ONE1F"], writes=["psW"])
            op("vector", lambda e, d0=dstt: e.tensor_copy(d0[:, :], psW[:, 0:8]),
               reads=["psW"], writes=[dstk])
        op("tensor", lambda e: e.matmul(psW[0:24, 8:16], ONE1F[0:1, 0:24], SS8[0:1, :],
                                        start=True, stop=True),
           reads=["SS8", "ONE1F"], writes=["psW"])
        op("vector", lambda e: e.tensor_copy(SEGS[:, :], psW[0:24, 8:16]), reads=["psW"], writes=["SEGS"])
        op("vector", lambda e: e.tensor_single_scalar(SEGS[:, :], SEGS[:, :], GS24T[:, 0:1], ALU.is_le),
           reads=["SEGS", "GS24T"], writes=["SEGS"])
        op("vector", lambda e: e.reduce_sum(GS24T[:, 0:1] if False else SEGS[:, 0:1], SEGS[:, :], axis=AX.X),
           reads=["SEGS"], writes=["SEGS"])
        op("vector", lambda e: e.tensor_scalar(SEGS[:, 0:1], SEGS[:, 0:1], -1.0, None, ALU.add),
           reads=["SEGS"], writes=["SEGS"])
        op("tensor", lambda e: e.transpose(psW[0:1, 16:40], SEGS[0:24, 0:1], IDF[0:24, 0:24]),
           reads=["SEGS", "IDF"], writes=["psW"])
        op("vector", lambda e: e.tensor_copy(ETABF[:, :], psW[0:1, 16:40]),
           reads=["psW"], writes=["ETABF"])
        op("tensor", lambda e: e.matmul(psW[:, 40:64], ONE1F[:, :], ETABF[0:1, :],
                                        start=True, stop=True),
           reads=["ETABF", "ONE1F"], writes=["psW"])
        op("vector", lambda e: e.tensor_copy(EGB[:, :], psW[:, 40:64]),
           reads=["psW"], writes=["EGB"])
        op("vector", lambda e: e.memset(TM96[:, :], 0), writes=["TM96"])
        op("gpsimd", lambda e: e.dma_start(
            out=tokmap_d.rearrange("(p c) x -> p (c x)", p=128), in_=TM96[:, :]),
           reads=["TM96"], writes=["tmz"], dma=True)
        allt = [f"tm{j}" for j in range(NT)]
        for j in range(NT):
            oh = OHA[:, j * 8:(j + 1) * 8]
            op("vector", lambda e, o=oh: e.tensor_mul(TMP8[:, :], o, SEGB[:, :]),
               reads=["OHA", "SEGB"], writes=["TMP8"])
            op("vector", lambda e: e.reduce_sum(RW[:, 0:1], TMP8[:, :], axis=AX.X),
               reads=["TMP8"], writes=["RW0"])
            op("vector", lambda e, o=oh: e.tensor_mul(TMP8[:, :], o, NVB[:, :]),
               reads=["OHA", "NVB"], writes=["TMP8"])
            op("vector", lambda e: e.reduce_sum(RW[:, 1:2], TMP8[:, :], axis=AX.X),
               reads=["TMP8"], writes=["RW1"])
            op("vector", lambda e, jj=j: e.tensor_single_scalar(RW[:, 2:3], RANKA[:, jj:jj + 1],
                                                                RW[:, 1:2], ALU.is_lt),
               reads=["RANKA", "RW1"], writes=["RW2"])
            op("vector", lambda e, jj=j: e.tensor_add(RW[:, 3:4], RANKA[:, jj:jj + 1], RW[:, 0:1]),
               reads=["RANKA", "RW0"], writes=["RW3"])
            op("vector", lambda e: e.tensor_scalar(RW[:, 3:4], RW[:, 3:4], -1048576.0, None, ALU.add),
               reads=["RW3"], writes=["RW3"])
            op("vector", lambda e: e.tensor_mul(RW[:, 3:4], RW[:, 3:4], RW[:, 2:3]),
               reads=["RW3", "RW2"], writes=["RW3"])
            op("vector", lambda e: e.tensor_scalar(RW[:, 3:4], RW[:, 3:4], 1048576.0, None, ALU.add),
               reads=["RW3"], writes=["RW3"])
            op("vector", lambda e: e.tensor_scalar(RW[:, 3:4], RW[:, 3:4], float(24 * 512 - 1), None, ALU.min),
               reads=["RW3"], writes=["RW3"])
            op("vector", lambda e, jj=j: e.tensor_copy(SLOTI[:, jj:jj + 1], RW[:, 3:4]),
               reads=["RW3"], writes=["SLOTI"])
            op("vector", lambda e, jj=j: e.tensor_mul(RW[:, 4:5], COMB[:, jj:jj + 1], RW[:, 2:3]),
               reads=["COMB", "RW2"], writes=["RW4"])
            op("vector", lambda e, jj=j: e.tensor_scalar(COEF[:, jj:jj + 1], RW[:, 4:5], 0.5, None, ALU.mult),
               reads=["RW4"], writes=["COEF"])
            op("vector", lambda e, jj=j: e.tensor_scalar(RW[:, 5:6], IOTA128[:, :], 1.0, float(jj * 128),
                                                         ALU.mult, ALU.add),
               reads=["IOTA128"], writes=["RW5"])
            op("vector", lambda e: e.tensor_copy(TMAPV[:, 0:1], RW[:, 5:6]),
               reads=["RW5", "TMAPV"], writes=["TMAPV"])
            op("gpsimd", lambda e, jj=j: e.indirect_dma_start(
                out=tokmap_d[:, :], out_offset=IOA(SLOTI[:, jj:jj + 1], 0),
                in_=TMAPV[:, :], in_offset=None),
               reads=["TMAPV", "SLOTI", "tmz"], writes=[f"tm{j}"], dma=True)
        allxln = [f"xln{j}" for j in range(NT)]
        for g in range(24):
            op("vector", lambda e, gg=g: e.tensor_scalar_mul(IDXWF[:, :], MULTW[:, :], EGB[:, gg:gg + 1]),
               reads=["MULTW", "EGB"], writes=["IDXWF"])
            op("vector", lambda e: e.tensor_add(IDXWF[:, :], IDXWF[:, :], BIDXW[:, :]),
               reads=["IDXWF", "BIDXW"], writes=["IDXWF"])
            op("vector", lambda e: e.tensor_copy(IDXWI[:, :], IDXWF[:, :]),
               reads=["IDXWF"], writes=["IDXWI"])
            op("vector", lambda e, gg=g: e.tensor_scalar(RW[:, 6:7], IOTA128[:, :], 1.0, None,
                                                         ALU.mult, accum_out=None)
               if False else e.tensor_scalar(RW[:, 6:7], EGB[:, gg:gg + 1], 128.0, None, ALU.mult),
               reads=["EGB"], writes=["RW6"])
            op("vector", lambda e: e.tensor_add(RW[:, 6:7], RW[:, 6:7], IOTA128[:, :]),
               reads=["RW6", "IOTA128"], writes=["RW6"])
            op("vector", lambda e: e.tensor_copy(IDXB[:, :], RW[:, 6:7]),
               reads=["RW6"], writes=["IDXB"])
            op("vector", lambda e, gg=g: e.tensor_copy(IDXB2[:, :], EGB[0:2, gg:gg + 1]),
               reads=["EGB"], writes=["IDXB2"])
            for c in range(8):
                op("gpsimd", lambda e, cc=c: e.indirect_dma_start(
                    out=WB1[:, cc * 4096:(cc + 1) * 4096], out_offset=None,
                    in_=moew1_d[:, :], in_offset=IOA(IDXWI[:, cc:cc + 1], 0)),
                   reads=["IDXWI", "moew1_d"], writes=[f"WB1_{c}"], dma=True)
            for f in range(32):
                op("gpsimd", lambda e, ff=f: e.indirect_dma_start(
                    out=WB2[:, ff * 1024:(ff + 1) * 1024], out_offset=None,
                    in_=moew2_d[:, :], in_offset=IOA(IDXWI[:, 8 + ff:9 + ff], 0)),
                   reads=["IDXWI", "moew2_d"], writes=[f"WB2_{f // 4}"], dma=True)
            op("gpsimd", lambda e: e.indirect_dma_start(
                out=B1CUR[:, :], out_offset=None,
                in_=b1T_d[:, :], in_offset=IOA(IDXB[:, 0:1], 0)),
               reads=["IDXB", "b1T_d"], writes=["B1CUR"], dma=True)
            op("gpsimd", lambda e: e.indirect_dma_start(
                out=B2CUR[:, :], out_offset=None,
                in_=b2w_d[:, :], in_offset=IOA(IDXB2[:, 0:1], 0)),
               reads=["IDXB2", "b2w_d"], writes=["B2CUR"], dma=True)
            wb1keys = [f"WB1_{r}" for r in range(8)]
            wb2keys = [f"WB2_{r}" for r in range(8)]
            for sub in range(2):
                s0 = g * 512 + sub * 256
                for k in range(2):
                    idxt = IDX4 if k == 0 else IDX4B
                    idxk = "IDX4" if k == 0 else "IDX4B"
                    op("scalar", lambda e, ss=s0 + k * 128, it=idxt: e.dma_start(
                        out=it[:, :], in_=tokmap_d[ss:ss + 128, :]),
                       reads=allt + ["tmz"], writes=[idxk], dma=True, queue="scalar")
                    op("gpsimd", lambda e, kk=k, it=idxt: e.indirect_dma_start(
                        out=EB[kk][:, :], out_offset=None,
                        in_=xln_d[:, :], in_offset=IOA(it[:, 0:1], 0)),
                       reads=allxln + [idxk, f"EB{k}"], writes=[f"EB{k}"], dma=True)
                    transpose_tile(f"EB{k}", EB[k], "XLT0", XLT[0], (lambda c: c * 256), k * 128)
                outps = {(0, 0): psS[:, 1024:1536], (0, 1): psS[:, 1536:2048],
                         (1, 0): psU[:, :], (1, 1): psV[:, :]}
                outpk = {(0, 0): "psS_b2", (0, 1): "psS_b3", (1, 0): "psU", (1, 1): "psV"}
                for half in range(2):
                    for fi in range(16):
                        f = half * 16 + fi
                        hp3 = (half * 16 + fi) % 3
                        ps = (psS[:, 0:256], psS[:, 512:768], psW[:, 0:256])[hp3]
                        pk = ("psS_b0", "psS_b1", "psW")[hp3]
                        def hmm(e, ff=f, p=ps):
                            ins = None
                            for c in range(8):
                                ins = e.matmul(p, WB1[:, c * 4096 + ff * 128: c * 4096 + ff * 128 + 128],
                                               XLT[0][:, c * 256:(c + 1) * 256],
                                               start=(c == 0), stop=(c == 7))
                            return ins
                        op("tensor", hmm, reads=["XLT0"] + wb1keys, writes=[pk])
                        op("scalar", lambda e, ff=f, fj=fi, p=ps: e.activation(
                            H[:, fj * 256:(fj + 1) * 256], p, AF.Gelu, bias=B1CUR[:, ff:ff + 1]),
                           reads=[pk, "B1CUR"], writes=[f"H{fi}"])
                    hkeys = [f"H{fi}" for fi in range(16)]
                    for k in range(2):
                        for dh in range(2):
                            ps = outps[(k, dh)]
                            pk = outpk[(k, dh)]
                            def omm(e, kk=k, dd=dh, p=ps, hh=half):
                                ins = None
                                if hh == 0:
                                    ins = e.matmul(p, ONE1B[:, :], B2CUR[0:1, dd * 512:(dd + 1) * 512],
                                                   start=True, stop=False)
                                for fi in range(16):
                                    f = hh * 16 + fi
                                    ins = e.matmul(p, H[:, fi * 256 + kk * 128: fi * 256 + kk * 128 + 128],
                                                   WB2[:, f * 1024 + dd * 512: f * 1024 + dd * 512 + 512],
                                                   start=False, stop=(hh == 1 and fi == 15))
                                return ins
                            rds = hkeys + wb2keys + ["ONE1B", "B2CUR"]
                            if half == 1:
                                rds = rds + [pk]
                            op("tensor", omm, reads=rds, writes=[pk])
                for k in range(2):
                    for dh in range(2):
                        ps = outps[(k, dh)]
                        pk = outpk[(k, dh)]
                        op("vector", lambda e, kk=k, dd=dh, p=ps: e.tensor_copy(
                            THF[:, kk * 1024 + dd * 512: kk * 1024 + (dd + 1) * 512], p),
                           reads=[pk], writes=[f"THF{k}"])
                    op("sync", lambda e, ss=s0 + k * 128, kk=k: e.dma_start(
                        out=ysort_d[ss:ss + 128, :], in_=THF[:, kk * 1024:(kk + 1) * 1024]),
                       reads=[f"THF{k}"], writes=[f"ys{g}"], dma=True)
        allys = [f"ys{g}" for g in range(24)]
        for j in range(NT):
            xi = j % 2
            op("scalar", lambda e, jj=j, t=X[xi]: e.dma_start(
                out=t[:, :], in_=xbuf[jj * 128:(jj + 1) * 128, :]),
               reads=[f"xbuf{j}"], writes=[f"X{xi}"], dma=True, queue="scalar")
            yk = "THF0" if xi == 0 else "THF1"
            yt = THF[:, 0:1024] if xi == 0 else THF[:, 1024:2048]
            op("gpsimd", lambda e, jj=j, y0=yt: e.indirect_dma_start(
                out=y0, out_offset=None,
                in_=ysort_d[:, :], in_offset=IOA(SLOTI[:, jj:jj + 1], 0)),
               reads=allys + ["SLOTI", yk], writes=[yk], dma=True)
            op("vector", lambda e, jj=j, t=X[xi], y0=yt: e.scalar_tensor_tensor(
                t[:, :], y0, COEF[:, jj:jj + 1], t[:, :], ALU.mult, ALU.add),
               reads=[yk, "COEF", f"X{xi}"], writes=[f"X{xi}"])
            stats_tile(f"X{xi}", X[xi], j)
            op("scalar", lambda e, jj=j, t=X[xi]: e.dma_start(
                out=dest[jj * 128:(jj + 1) * 128, :], in_=t[:, :]),
               reads=[f"X{xi}"], writes=[f"{destname}{j}"], dma=True, queue="scalar")

    load_consts()
    if _STAGES >= 1:
        prep_l0()
        prep_attn_l0()
    if _STAGES >= 2:
        phase_load_x()
    if _STAGES >= 1:
        if _STAGES >= 6:
            prep_moe()
        if _STAGES >= 7 and _L1P >= 1:
            prep_l1()
    phases = []
    if _STAGES >= 3:
        phases.append(phase_l0ffn)
    if _STAGES >= 4:
        phases.append(phase_l0attn)
    if _STAGES >= 5:
        phases.append(phase_l0ffn)
    if _STAGES >= 6:
        phases.append(lambda d, n: phase_moe(d, n, cc_in, cc_out, 0))
    if _STAGES >= 7:
        if _L1P >= 5:
            phases.append(lambda d, n: (phase_l1pre(), phase_l1attn(), phase_l1ffn(d, n)))
        elif _L1P >= 1:
            phases.append(lambda d, n: (phase_l1pre(), phase_l1attn(), store_dummy(d, n)))
        else:
            phases.append(lambda d, n: store_dummy(d, n))
    if _STAGES >= 8:
        phases.append(lambda d, n: phase_moe(d, n, cc_in2, cc_out2, 1))
    for pi, ph in enumerate(phases):
        last = (pi == len(phases) - 1)
        ph(y_out if last else xbuf, "y" if last else "xbuf")

    with nc.Block() as block:
        @block.sync
        def _(e):
            em.replay("sync", e)

        @block.scalar
        def _(e):
            em.replay("scalar", e)

        @block.vector
        def _(e):
            em.replay("vector", e)

        @block.tensor
        def _(e):
            em.replay("tensor", e)

        @block.gpsimd
        def _(e):
            em.replay("gpsimd", e)
            em.finish("gpsimd", e)

    return nc


WSHAPES = {
    "L0_attn_Wq": (D, D), "L0_attn_bq": (D,), "L0_attn_Wk": (D, D), "L0_attn_bk": (D,),
    "L0_attn_Wv": (D, D), "L0_attn_bv": (D,), "L0_attn_Wo": (D, D), "L0_attn_bo": (D,),
    "L0_ffn_W1": (4096, D), "L0_ffn_b1": (4096,), "L0_ffn_W2": (D, 4096), "L0_ffn_b2": (D,),
    "L1_attn_wq": (D, D), "L1_attn_wk": (D, D), "L1_attn_wv": (D, D),
    "L1_attn_w1": (4096, D), "L1_attn_b1": (4096,), "L1_attn_w2": (D, 4096), "L1_attn_b2": (D,),
    "L1_attn_g1": (D,), "L1_attn_be1": (D,), "L1_attn_g2": (D,), "L1_attn_be2": (D,),
    "L1_ffn_wg": (D, 8), "L1_ffn_W1": (8, D, 4096), "L1_ffn_b1": (8, 4096),
    "L1_ffn_W2": (8, 4096, D), "L1_ffn_b2": (8, D),
}


def kernel(**inputs):
    x_full = np.ascontiguousarray(inputs["L0_attn_x"].reshape(32 * S, D))
    shards = [x_full[c * T:(c + 1) * T] for c in range(NCORE)]
    wmap = {}
    for name, shape in WSHAPES.items():
        arr = np.ascontiguousarray(inputs[name])
        if arr.ndim > 2 or name in ("L1_ffn_W1", "L1_ffn_W2", "L1_ffn_b1", "L1_ffn_b2"):
            arr = arr.reshape(arr.shape[0] * arr.shape[1], -1) if arr.ndim == 3 else arr
        wmap[name] = arr
    shapes = {k: v.shape for k, v in wmap.items()}
    nc = build(shapes)
    in_maps = []
    for c in range(NCORE):
        m = {"x": shards[c]}
        m.update(wmap)
        in_maps.append(m)
    res = run_bass_kernel_spmd(nc, in_maps, core_ids=list(range(NCORE)))
    out = np.concatenate([res.results[c]["y"] for c in range(NCORE)], axis=0)
    return out.reshape(32, S, D)
